# version 05
# baseline (speedup 1.0000x reference)
import math
from contextlib import ExitStack

import numpy as np

import concourse.bass as bass
import concourse.mybir as mybir

F32 = mybir.dt.float32
BF16 = mybir.dt.bfloat16
ALU = mybir.AluOpType
ACT = mybir.ActivationFunctionType
AX = mybir.AxisListType

NCORES = 8
B, L, D, H = 4, 4096, 1024, 8
E = 128
POS = L // NCORES
TLOC = B * POS
NT = TLOC // 128
CH = 256
DH = 4096
EPS = 1e-5

GAMMAS = 1.0 - np.exp(np.linspace(math.log(1.0 / 32.0), math.log(1.0 / 512.0), H))



def _xpos_tables():
    pos = np.arange(L, dtype=np.float64)
    base = (np.arange(0, E, 2, dtype=np.float64) + 0.4 * E) / (1.4 * E)
    scale = base[None, :] ** (pos[:, None] / 512.0)
    inv_freq = 1.0 / 10000.0 ** (np.arange(64, dtype=np.float64) / 64.0)
    sinus = pos[:, None] * inv_freq[None, :]
    sin, cos = np.sin(sinus), np.cos(sinus)
    return cos * scale, sin * scale, cos / scale, sin / scale


def host_prepare(inputs):
    f32 = np.float32
    perm = np.concatenate([np.arange(0, E, 2), np.arange(1, E, 2)])
    perm_swap = np.concatenate([perm[64:], perm[:64]])

    WQ = np.asarray(inputs["L0_attn_W_Q"], f32)
    WK = np.asarray(inputs["L0_attn_W_K"], f32)
    WV = np.asarray(inputs["L0_attn_W_V"], f32)
    wq = np.ascontiguousarray(
        np.concatenate([WQ[:, :, perm], WQ[:, :, perm_swap]], axis=2))
    wkv = np.ascontiguousarray(np.concatenate([WK[:, :, perm], WV], axis=2))

    def dcmaj(w):
        N = w.shape[1]
        return np.ascontiguousarray(
            w.reshape(8, 128, N).transpose(1, 0, 2).reshape(128, 8 * N))

    wg = dcmaj(np.asarray(inputs["L0_attn_W_G"], f32))
    wo = dcmaj(np.asarray(inputs["L0_attn_W_O"], f32))
    gnw = np.ascontiguousarray(np.broadcast_to(
        np.asarray(inputs["L0_attn_gn_weight"], f32)[None, :], (128, D)))
    gnb = np.ascontiguousarray(np.broadcast_to(
        np.asarray(inputs["L0_attn_gn_bias"], f32)[None, :], (128, D)))

    def lhst_blocks(w, nm, nin):
        nic = nin // 128
        return np.ascontiguousarray(
            w.reshape(nm, 128, nic, 128).transpose(0, 3, 2, 1).reshape(nm, 128, nin))

    w1t = lhst_blocks(np.asarray(inputs["L0_ffn_w1_w"], f32), 32, 1024)
    w2t = lhst_blocks(np.asarray(inputs["L0_ffn_w2_w"], f32), 32, 1024)
    h1t = lhst_blocks(np.asarray(inputs["L0_ffn_h1_w"], f32), 32, 4096)
    h2t = lhst_blocks(np.asarray(inputs["L0_ffn_h2_w"], f32), 32, 4096)
    w3t = lhst_blocks(np.asarray(inputs["L0_ffn_w3_w"], f32), 8, 4096)

    def pscale(s):
        n = np.asarray(s).shape[0] // 128
        return np.ascontiguousarray(np.asarray(s, f32).reshape(n, 128).T)

    w1s = pscale(inputs["L0_ffn_w1_s"]); w2s = pscale(inputs["L0_ffn_w2_s"])
    h1s = pscale(inputs["L0_ffn_h1_s"]); h2s = pscale(inputs["L0_ffn_h2_s"])
    w3s = pscale(inputs["L0_ffn_w3_s"])

    kvq = np.stack([np.asarray(inputs["L1_attn_W_k"], f32),
                    np.asarray(inputs["L1_attn_W_v"], f32),
                    np.asarray(inputs["L1_attn_W_q"], f32)], 0)
    wkvq2 = np.ascontiguousarray(
        kvq.transpose(2, 0, 1)
        .reshape(8, 128, 3, 1024)
        .transpose(1, 0, 2, 3)
        .reshape(128, 8 * 3 * 1024))

    def dlhst(w):
        return np.ascontiguousarray(
            w.T.reshape(8, 128, 1024).transpose(1, 0, 2).reshape(128, 8192))

    wr = dlhst(np.asarray(inputs["L1_ffn_W_r"], f32))
    wkd = dlhst(np.asarray(inputs["L1_ffn_W_k"], f32))
    wvd = dlhst(np.asarray(inputs["L1_ffn_W_v"], f32))
    mur = pscale(inputs["L1_ffn_mu_r"])
    muk = pscale(inputs["L1_ffn_mu_k"])

    ident = np.eye(128, dtype=f32)
    r = np.arange(CH); s = np.arange(128)
    m0 = (s[:, None] <= r[None, :]).astype(f32)
    m1 = ((s[:, None] + 128) <= r[None, :]).astype(f32)

    cq, sq, ck, sk = _xpos_tables()
    X = np.asarray(inputs["L0_attn_X"], f32)

    shared = dict(wq=wq, wkv=wkv, wg=wg, wo=wo, gnw=gnw, gnb=gnb,
                  w1t=w1t, w2t=w2t, h1t=h1t, h2t=h2t, w3t=w3t,
                  w1s=w1s, w2s=w2s, h1s=h1s, h2s=h2s, w3s=w3s,
                  wkvq2=wkvq2, wr=wr, wkd=wkd, wvd=wvd, mur=mur, muk=muk,
                  ident=ident, m0=m0, m1=m1)

    i_of_p = np.arange(128) % 64
    sgn = np.where(np.arange(128) < 64, -1.0, 1.0)

    in_maps = []
    for c in range(NCORES):
        p0 = c * POS
        posg = np.arange(p0, p0 + POS)
        x0 = np.ascontiguousarray(X[:, p0:p0 + POS, :].reshape(TLOC, D))

        cqfm = np.ascontiguousarray(cq[posg][:, i_of_p].T).astype(f32)
        sqfm = np.ascontiguousarray((sq[posg][:, i_of_p] * sgn[None, :]).T).astype(f32)

        ckh = np.zeros((128, H * 512), np.float64)
        skh = np.zeros((128, H * 512), np.float64)
        for h in range(H):
            g = GAMMAS[h]
            for pt in range(4):
                pg = posg[pt * 128:(pt + 1) * 128]
                dec = g ** (CH - (pg % CH))
                c0 = h * 512 + pt * 128
                ckh[:, c0:c0 + 128] = ck[pg][:, i_of_p] * dec[:, None]
                skh[:, c0:c0 + 128] = sk[pg][:, i_of_p] * sgn[None, :] * dec[:, None]
        ckh = ckh.astype(f32); skh = skh.astype(f32)

        ysc = np.zeros((128, 32), np.float64)
        for h in range(H):
            for pt in range(4):
                pg = posg[pt * 128:(pt + 1) * 128]
                ysc[:, h * 4 + pt] = GAMMAS[h] ** ((pg % CH).astype(np.float64) - CH)
        ysc = ysc.astype(f32)

        wpref = np.zeros((128, 64), np.float64)
        for cp in range(c):
            for h in range(H):
                wpref[:, cp * 8 + h] = GAMMAS[h] ** (2.0 * CH * (c - 1 - cp))
        wpref = wpref.astype(f32)

        haloflag = np.zeros((128, 8), f32)
        if c > 0:
            haloflag[:, c - 1] = 1.0

        m = dict(shared)
        m.update(x0=x0, cqfm=cqfm, sqfm=sqfm, ckh=ckh, skh=skh, ysc=ysc,
                 wpref=wpref, haloflag=haloflag)
        in_maps.append(m)
    return in_maps



def swap_ap(ap128):
    t = ap128.tensor
    pstep = ap128.ap[0][0]
    return bass.AP(t, ap128.offset + 64, [[pstep, 128], [-64, 2], [1, 64]])


class KB:

    def __init__(self, nc, es):
        self.nc = nc
        self.es = es
        self.sems = {}
        self.cnt = {}
        for nm in ["ld", "wld", "mm", "ev", "sc", "st", "cc", "tp", "aux",
                   "q", "sm", "ym", "tst", "gp", "cst"]:
            self.sems[nm] = es.enter_context(nc.semaphore(nm))
            self.cnt[nm] = 0


def build_program(stop_phase="D"):
    nc = bass.Bass("TRN2", target_bir_lowering=False)

    def din(name, shape, dtype=F32):
        return nc.dram_tensor(name, list(shape), dtype, kind="ExternalInput").ap()

    T = {}
    T["x0"] = din("x0", [TLOC, D])
    T["wq"] = din("wq", [H, D, 256]); T["wkv"] = din("wkv", [H, D, 256])
    T["wg"] = din("wg", [128, 8 * D]); T["wo"] = din("wo", [128, 8 * D])
    T["gnw"] = din("gnw", [128, D]); T["gnb"] = din("gnb", [128, D])
    T["w1t"] = din("w1t", [32, 128, 1024]); T["w2t"] = din("w2t", [32, 128, 1024])
    T["h1t"] = din("h1t", [32, 128, 4096]); T["h2t"] = din("h2t", [32, 128, 4096])
    T["w3t"] = din("w3t", [8, 128, 4096])
    T["w1s"] = din("w1s", [128, 32]); T["w2s"] = din("w2s", [128, 32])
    T["h1s"] = din("h1s", [128, 32]); T["h2s"] = din("h2s", [128, 32])
    T["w3s"] = din("w3s", [128, 8])
    T["wkvq2"] = din("wkvq2", [128, 24576])
    T["wr"] = din("wr", [128, 8192]); T["wkd"] = din("wkd", [128, 8192])
    T["wvd"] = din("wvd", [128, 8192])
    T["mur"] = din("mur", [128, 8]); T["muk"] = din("muk", [128, 8])
    T["ident"] = din("ident", [128, 128])
    T["m0"] = din("m0", [128, 256]); T["m1"] = din("m1", [128, 256])
    T["cqfm"] = din("cqfm", [128, 512]); T["sqfm"] = din("sqfm", [128, 512])
    T["ckh"] = din("ckh", [128, H * 512]); T["skh"] = din("skh", [128, H * 512])
    T["ysc"] = din("ysc", [128, 32]); T["wpref"] = din("wpref", [128, 64])
    T["haloflag"] = din("haloflag", [128, 8])

    y = nc.dram_tensor("y", [TLOC, D], F32, kind="ExternalOutput").ap()

    T["cc_a_in"] = nc.dram_tensor("cc_a_in", [128, H * 512], F32).ap()
    T["cc_a_out"] = nc.dram_tensor("cc_a_out", [NCORES * 128, H * 512], F32).ap()
    T["cc_h_in"] = nc.dram_tensor("cc_h_in", [B, D], F32).ap()
    T["cc_h_out"] = nc.dram_tensor("cc_h_out", [NCORES * B, D], F32).ap()
    T["wkvq2_bf"] = nc.dram_tensor("wkvq2_bf", [128, 24576], BF16).ap()
    T["w1t_bf"] = nc.dram_tensor("w1t_bf", [32, 128, 1024], BF16).ap()
    T["w2t_bf"] = nc.dram_tensor("w2t_bf", [32, 128, 1024], BF16).ap()
    T["h1t_bf"] = nc.dram_tensor("h1t_bf", [32, 128, 4096], BF16).ap()
    T["h2t_bf"] = nc.dram_tensor("h2t_bf", [32, 128, 4096], BF16).ap()
    T["w3t_bf"] = nc.dram_tensor("w3t_bf", [8, 128, 4096], BF16).ap()
    T["wg_bf"] = nc.dram_tensor("wg_bf", [128, 8192], BF16).ap()
    T["wo_bf"] = nc.dram_tensor("wo_bf", [128, 8192], BF16).ap()

    with ExitStack() as es:
        kb = KB(nc, es)
        kb.T = T
        kb.y = y

        kb.stream = es.enter_context(nc.sbuf_tensor("stream", [128, NT * 1024], BF16))
        kb.fmbuf = es.enter_context(nc.sbuf_tensor("fmbuf", [128, NT * 1024], BF16))
        kb.idb = es.enter_context(nc.sbuf_tensor("idb", [128, 128], BF16))
        kb.lnv = es.enter_context(nc.sbuf_tensor("lnv", [128, 32], F32))
        kb.trash = es.enter_context(nc.sbuf_tensor("trash", [128, 2048], BF16))
        kb.r32 = es.enter_context(nc.sbuf_tensor("r32", [128, 2048], F32))

        emit_phase_a(kb)
        if stop_phase == "A":
            _debug_out(kb)
            return nc
        emit_phase_b(kb)
        if stop_phase == "B":
            _debug_out(kb)
            return nc
        emit_phase_c(kb)
        if stop_phase == "C":
            _debug_out(kb)
            return nc
        emit_phase_d(kb)
    return nc


def _debug_out(kb):
    nc, sems, cnt = kb.nc, kb.sems, kb.cnt
    with nc.Block("dbgout") as blk:
        @blk.gpsimd
        def _(g):
            for t in range(NT):
                g.dma_start(out=kb.y[t * 128:(t + 1) * 128, :],
                            in_=kb.stream[:, t * 1024:(t + 1) * 1024]).then_inc(sems["st"], 16)
            g.wait_ge(sems["st"], cnt["st"] + 16 * NT)
    cnt["st"] += 16 * NT



def emit_tm_to_fm(kb, name):
    nc, sems, cnt = kb.nc, kb.sems, kb.cnt
    with nc.psum_tensor("ptr_" + name, [128, 1024], BF16) as ptr, \
         nc.psum_tensor("ptr2_" + name, [128, 1024], BF16) as ptr2:
        with nc.Block(name) as blk:
            @blk.tensor
            def _(t):
                for grp in range(16):
                    dc, tg = grp // 2, grp % 2
                    pp = ptr if grp % 2 == 0 else ptr2
                    if grp >= 2:
                        t.wait_ge(sems["tp"], cnt["tp"] + grp - 1)
                    for tt in range(8):
                        t0 = tg * 8 + tt
                        ins = t.transpose(pp[:, tt * 128:(tt + 1) * 128],
                                          kb.stream[:, t0 * 1024 + dc * 128:
                                                    t0 * 1024 + dc * 128 + 128],
                                          kb.idb[:, :])
                    ins.then_inc(sems["mm"], 1)

            @blk.vector
            def _(v):
                for grp in range(16):
                    dc, tg = grp // 2, grp % 2
                    pp = ptr if grp % 2 == 0 else ptr2
                    v.wait_ge(sems["mm"], cnt["mm"] + grp + 1)
                    v.tensor_copy(kb.fmbuf[:, dc * 2048 + tg * 1024:
                                           dc * 2048 + (tg + 1) * 1024],
                                  pp[:, :]).then_inc(sems["tp"], 1)
    cnt["mm"] += 16
    cnt["tp"] += 16



def ln_v1(v, kb, resid, sl=0):
    o = sl * 16
    return v.tensor_scalar(kb.trash[:, sl * 1024:(sl + 1) * 1024], resid, 1.0,
                           None, ALU.mult, ALU.add,
                           accum_out=kb.lnv[:, o + 0:o + 1])


def ln_s1(s, kb, resid, sl=0):
    o = sl * 16
    return s.activation(kb.trash[:, sl * 1024:(sl + 1) * 1024], resid, ACT.Square,
                        accum_out=kb.lnv[:, o + 1:o + 2])


def ln_v2(v, kb, sl=0):
    o = sl * 16
    lnv = kb.lnv
    v.tensor_scalar(lnv[:, o + 2:o + 3], lnv[:, o + 0:o + 1], 1.0 / 1024.0,
                    None, ALU.mult)
    v.tensor_scalar(lnv[:, o + 3:o + 4], lnv[:, o + 1:o + 2], 1.0 / 1024.0,
                    None, ALU.mult)
    v.drain()
    v.tensor_tensor(lnv[:, o + 7:o + 8], lnv[:, o + 2:o + 3],
                    lnv[:, o + 2:o + 3], ALU.mult)
    v.drain()
    v.tensor_tensor(lnv[:, o + 3:o + 4], lnv[:, o + 3:o + 4],
                    lnv[:, o + 7:o + 8], ALU.subtract)
    v.drain()
    v.tensor_scalar(lnv[:, o + 3:o + 4], lnv[:, o + 3:o + 4], 0.0, EPS,
                    ALU.max, ALU.add)
    v.drain()
    return v.reciprocal(lnv[:, o + 4:o + 5], lnv[:, o + 3:o + 4])


def ln_s2_sqrt(s, kb, sl=0):
    o = sl * 16
    return s.activation(kb.lnv[:, o + 5:o + 6], kb.lnv[:, o + 4:o + 5], ACT.Sqrt)


def ln_v3(v, kb, sl=0):
    o = sl * 16
    v.drain()
    return v.tensor_scalar(kb.lnv[:, o + 6:o + 7], kb.lnv[:, o + 2:o + 3],
                           kb.lnv[:, o + 5:o + 6], -1.0, ALU.mult, ALU.mult)


def ln_apply(s, kb, resid, out, sl=0):
    o = sl * 16
    s.drain()
    return s.activation(out, resid, ACT.Identity, bias=kb.lnv[:, o + 6:o + 7],
                        scale=kb.lnv[:, o + 5:o + 6])



def emit_kv_head_T(t, kb, h, wkv_sb, pkv, pkv2, ktm, vtm_h, ev_base, mm_sem, ev_sem):
    sems = kb.sems
    for tt in range(NT):
        ps = pkv if tt % 2 == 0 else pkv2
        t.wait_ge(ev_sem, ev_base + max(0, tt - 1))
        for dc in range(8):
            ins = t.matmul(ps[:, :],
                           kb.fmbuf[:, dc * 2048 + tt * 128:dc * 2048 + tt * 128 + 128],
                           wkv_sb[:, dc * 256:(dc + 1) * 256],
                           start=(dc == 0), stop=(dc == 7))
        ins.then_inc(mm_sem, 1)


def emit_kv_head_V(v, kb, h, pkv, pkv2, ktm, vtm_t, vcol0, ckh_sb, skh_sb,
                   ktmp1, ktmp2, mm_base, ev_sem, mm_sem):
    for tt in range(NT):
        ps = pkv if tt % 2 == 0 else pkv2
        pt = tt % 4
        v.wait_ge(mm_sem, mm_base + tt + 1)
        ck_ap = ckh_sb[:, h * 512 + pt * 128:h * 512 + pt * 128 + 128]
        sk_ap = skh_sb[:, h * 512 + pt * 128:h * 512 + pt * 128 + 128]
        v.tensor_tensor(ktmp1[:, :], ps[:, 0:128], ck_ap, ALU.mult)
        v.tensor_tensor(ktmp2[:, :], swap_ap(ps[:, 0:128]), sk_ap, ALU.mult)
        v.tensor_tensor(ktm[:, tt * 128:(tt + 1) * 128], ktmp1[:, :], ktmp2[:, :],
                        ALU.add)
        v.tensor_copy(vtm_t[:, vcol0 + tt * 128:vcol0 + (tt + 1) * 128],
                      ps[:, 128:256]).then_inc(ev_sem, 1)


SUBSTOP = "FULL"
DBG = False


def emit_phase_a(kb):
    nc, sems, cnt, T = kb.nc, kb.sems, kb.cnt, kb.T

    with ExitStack() as aes:
        sb = lambda name, shp, dt=BF16: aes.enter_context(nc.sbuf_tensor(name, shp, dt))
        cqfm_sb = sb("cqfm_sb", [128, 512])
        sqfm_sb = sb("sqfm_sb", [128, 512])
        ckh_sb = sb("ckh_sb", [128, H * 512])
        skh_sb = sb("skh_sb", [128, H * 512])
        m0_sb = sb("m0_sb", [128, 256]); m1_sb = sb("m1_sb", [128, 256])
        gnw_sb = sb("gnw_sb", [128, D])
        gnb_sb = sb("gnb_sb", [128, D])
        ysc_sb = sb("ysc_sb", [128, 32], F32)
        wpref_sb = sb("wpref_sb", [128, 64], F32)
        yn = sb("yn", [128, NT * 1024])
        tinit_all = sb("tinit_all", [128, H * B * 128])
        tstate = sb("tstate", [128, B * 128], F32)
        tstate_bf = sb("tstate_bf", [128, B * 128])
        asb = sb("asb", [128, B * 128], F32)
        sm_sb = sb("sm_sb", [128, 2 * 256])

        ld0, wld0 = cnt["ld"], cnt["wld"]
        with nc.Block("A0a") as blk:
            @blk.sync
            def _(s):
                for ap, src in [(ysc_sb, "ysc"), (wpref_sb, "wpref")]:
                    s.dma_start(out=ap[:, :], in_=T[src]).then_inc(sems["ld"], 16)
                s.wait_ge(sems["ld"], ld0 + 2 * 16)

            @blk.gpsimd
            def _(g):
                g.dma_start(out=kb.stream[:, :].rearrange("p (t d) -> p t d", t=NT),
                            in_=T["x0"].rearrange("(t p) d -> p t d", p=128)
                            ).then_inc(sems["wld"], 16)
                for ap, src in [(kb.idb, "ident"), (ckh_sb, "ckh"), (skh_sb, "skh"),
                                (m0_sb, "m0"), (m1_sb, "m1"),
                                (gnw_sb, "gnw"), (gnb_sb, "gnb"),
                                (cqfm_sb, "cqfm"), (sqfm_sb, "sqfm")]:
                    g.dma_start(out=ap[:, :], in_=T[src]).then_inc(sems["wld"], 16)
                g.wait_ge(sems["wld"], wld0 + 10 * 16)
        cnt["ld"] += 2 * 16
        cnt["wld"] += 10 * 16

        emit_tm_to_fm(kb, "A0b")
        if SUBSTOP == "A0":
            return

        mm0, ev0, aux0, wld0, st0 = (cnt["mm"], cnt["ev"], cnt["aux"],
                                     cnt["wld"], cnt["st"])
        with ExitStack() as a1s:
            sb1 = lambda name, shp, dt=BF16: a1s.enter_context(
                nc.sbuf_tensor(name, shp, dt))
            ktm = sb1("ktm_a1", [128, 2048])
            vtm1 = sb1("vtm_a1", [128, 2048])
            ktmp1 = sb1("ktmp1_a1", [128, 128], F32)
            ktmp2 = sb1("ktmp2_a1", [128, 128], F32)
            wkv_sb = sb1("wkv_a1", [128, 2048])
            i2a = sb1("i2a", [128, 256], F32)
            with (nc.psum_tensor("pkv", [128, 256], F32) as pkv,
                  nc.psum_tensor("pkv2", [128, 256], F32) as pkv2,
                  nc.psum_tensor("pinc", [128, 128], F32) as pinc,
                  nc.psum_tensor("pinc2", [128, 128], F32) as pinc2):
                with nc.Block("A1") as blk:
                    @blk.gpsimd
                    def _(g):
                        for h in range(H):
                            if h >= 1:
                                g.wait_ge(sems["mm"], mm0 + h * 16)
                            g.dma_start(out=wkv_sb[:, :].rearrange(
                                            "p (dc n) -> p dc n", dc=8),
                                        in_=T["wkv"][h].rearrange(
                                            "(dc p) n -> p dc n", p=128)
                                        ).then_inc(sems["wld"], 16)

                    @blk.tensor
                    def _(t):
                        for h in range(H):
                            t.wait_ge(sems["wld"], wld0 + (h + 1) * 16)
                            emit_kv_head_T(t, kb, h, wkv_sb, pkv, pkv2, ktm, vtm1,
                                           ev0 + h * 25, sems["mm"], sems["ev"])
                            for b in range(B):
                                for ci in range(2):
                                    cx = b * 2 + ci
                                    pp = pinc if cx % 2 == 0 else pinc2
                                    t.wait_ge(sems["ev"],
                                              ev0 + h * 25 + 16 + max(0, cx - 1))
                                    for sub in range(2):
                                        tt = b * 4 + ci * 2 + sub
                                        ins = t.matmul(pp[:, :],
                                                       ktm[:, tt * 128:(tt + 1) * 128],
                                                       vtm1[:, tt * 128:(tt + 1) * 128],
                                                       start=(sub == 0), stop=(sub == 1))
                                    ins.then_inc(sems["aux"], 1)

                    @blk.vector
                    def _(v):
                        for h in range(H):
                            emit_kv_head_V(v, kb, h, pkv, pkv2, ktm, vtm1, 0,
                                           ckh_sb, skh_sb, ktmp1, ktmp2,
                                           mm0 + h * 16, sems["ev"], sems["mm"])
                            gC = float(GAMMAS[h] ** CH)
                            for b in range(B):
                                for ci in range(2):
                                    cx = b * 2 + ci
                                    pp = pinc if cx % 2 == 0 else pinc2
                                    v.wait_ge(sems["aux"], aux0 + h * 8 + cx + 1)
                                    v.tensor_scalar(i2a[:, ci * 128:(ci + 1) * 128],
                                                    pp[:, :], gC, None,
                                                    ALU.mult).then_inc(sems["ev"], 1)
                                v.drain()
                                ins = v.scalar_tensor_tensor(
                                    asb[:, b * 128:(b + 1) * 128],
                                    i2a[:, 0:128], gC, i2a[:, 128:256],
                                    ALU.mult, ALU.add)
                            ins.then_inc(sems["ev"], 1)

                    @blk.sync
                    def _(s):
                        for h in range(H):
                            s.wait_ge(sems["ev"], ev0 + (h + 1) * 25)
                            s.dma_start(out=T["cc_a_in"][:, h * 512:(h + 1) * 512],
                                        in_=asb[:, :]).then_inc(sems["st"], 16)
                        s.wait_ge(sems["st"], st0 + H * 16)
        cnt["mm"] += H * 16
        cnt["aux"] += H * 8
        cnt["ev"] += H * 25
        cnt["wld"] += H * 16
        cnt["st"] += H * 16
        if SUBSTOP == "A1":
            return

        mm0, ev0, aux0, q0, sm0, ym0, tst0, st0, cc0, wld0, sc0, tp0, gp0 = (
            cnt["mm"], cnt["ev"], cnt["aux"], cnt["q"], cnt["sm"], cnt["ym"],
            cnt["tst"], cnt["st"], cnt["cc"], cnt["wld"], cnt["sc"], cnt["tp"],
            cnt["gp"])
        DO_HEADS = SUBSTOP not in ("A2coll",)
        DO_TR = SUBSTOP not in ("A2coll", "A2h0")
        DO_INC = DO_TR and SUBSTOP != "A2h0t"
        DO_QP = DO_INC and SUBSTOP != "A2h0i"
        DO_CHUNKS = DO_QP and SUBSTOP not in ("A2h1",)
        NH = H if DO_HEADS else 0
        MMH = 16 + (2 if DO_TR else 0) + (4 if DO_QP else 0)
        EVH = 16 + (2 if DO_TR else 0) + (8 if DO_INC else 0) + \
            (2 if DO_CHUNKS else 0)
        with ExitStack() as a2s:
            sb2 = lambda name, shp, dt=BF16: a2s.enter_context(
                nc.sbuf_tensor(name, shp, dt))
            ktm = sb2("ktm_a2", [128, 2048])
            ktmp1 = sb2("ktmp1_a2", [128, 128], F32)
            ktmp2 = sb2("ktmp2_a2", [128, 128], F32)
            wkv_sb = sb2("wkv_a2", [128, 2048])
            wq_sb = sb2("wq_a2", [128, 2048])
            qfm = sb2("qfm", [128, 2 * 2048])
            qtmp1 = sb2("qtmp1", [128, 512], F32)
            qtmp2 = sb2("qtmp2", [128, 512], F32)
            kfm2 = sb2("kfm2", [128, 2 * 2048])
            vtm2 = sb2("vtm2", [128, 2 * 2048])
            i2 = sb2("i2", [128, 8 * 128])
            pfx = sb2("pfx", [128, 2 * 512], F32)
            ystage = sb2("ystage", [128, 2048], F32)
            ysq = kb.r32
            gstat = sb2("gstat", [128, 112], F32)
            with (nc.psum_tensor("pkv_a2", [128, 256], F32) as pkv,
                  nc.psum_tensor("pkv2_a2", [128, 256], F32) as pkv2,
                  nc.psum_tensor("ptr1", [128, 1024], BF16) as ptr1,
                  nc.psum_tensor("pq1", [128, 512], F32) as pq1,
                  nc.psum_tensor("pq2", [128, 512], F32) as pq2,
                  nc.psum_tensor("psc", [128, 512], F32) as psc,
                  nc.psum_tensor("py12", [128, 256], F32) as py12):
                pinc, pinc2 = pq1, pq2
                with nc.Block("A2") as blk:
                    @blk.gpsimd
                    def _(g):
                        g.wait_ge(sems["st"], st0)
                        g.collective_compute(
                            "AllGather", ALU.bypass,
                            replica_groups=[list(range(NCORES))],
                            ins=[T["cc_a_in"].opt()], outs=[T["cc_a_out"].opt()],
                        ).then_inc(sems["cc"], 1)
                        for h in range(NH):
                            if h >= 1:
                                g.wait_ge(sems["mm"], mm0 + (h - 1) * MMH + 16)
                            if h >= 2 and DO_CHUNKS:
                                g.wait_ge(sems["ym"], ym0 + (h - 2) * 8 + 8)
                            g.dma_start(out=wkv_sb[:, :].rearrange(
                                            "p (dc n) -> p dc n", dc=8),
                                        in_=T["wkv"][h].rearrange(
                                            "(dc p) n -> p dc n", p=128)
                                        ).then_inc(sems["wld"], 16)
                            g.dma_start(out=wq_sb[:, :].rearrange(
                                            "p (dc n) -> p dc n", dc=8),
                                        in_=T["wq"][h].rearrange(
                                            "(dc p) n -> p dc n", p=128)
                                        ).then_inc(sems["wld"], 16)
                        g.dma_start(out=T["wg_bf"], in_=T["wg"]).then_inc(sems["cst"], 16)
                        g.dma_start(out=T["wo_bf"], in_=T["wo"]).then_inc(sems["cst"], 16)
                        for srcn, dstn, nm in [("w1t", "w1t_bf", 32),
                                               ("w2t", "w2t_bf", 32),
                                               ("w3t", "w3t_bf", 8),
                                               ("h1t", "h1t_bf", 32)]:
                            for m in range(nm):
                                g.dma_start(out=T[dstn][m],
                                            in_=T[srcn][m]).then_inc(sems["cst"], 16)

                    @blk.sync
                    def _(s):
                        s.wait_ge(sems["cc"], cc0 + 1)
                        for h in range(H):
                            for cp in range(NCORES):
                                li = h * NCORES + cp
                                pslot = pfx[:, (li % 2) * 512:(li % 2 + 1) * 512]
                                s.wait_ge(sems["tst"], tst0 + max(0, li - 1))
                                s.dma_start(
                                    out=pslot,
                                    in_=T["cc_a_out"][cp * 128:(cp + 1) * 128,
                                                      h * 512:(h + 1) * 512]
                                    ).then_inc(sems["st"], 16)

                    @blk.tensor
                    def _(t):
                        for h in range(NH):
                            t.wait_ge(sems["wld"], wld0 + h * 32 + 16)
                            emit_kv_head_T(t, kb, h, wkv_sb, pkv, pkv2, ktm,
                                           vtm2, ev0 + h * EVH, sems["mm"],
                                           sems["ev"])
                            for grp in range(2 if DO_TR else 0):
                                t.wait_ge(sems["ev"], ev0 + h * EVH + (grp + 1) * 8)
                                if grp == 1:
                                    t.wait_ge(sems["ev"], ev0 + h * EVH + 17)
                                for tt in range(8):
                                    t0 = grp * 8 + tt
                                    ins = t.transpose(
                                        pp_slice(ptr1, tt),
                                        ktm[:, t0 * 128:(t0 + 1) * 128],
                                        kb.idb[:, :])
                                ins.then_inc(sems["mm"], 1)
                            for b in range(B if DO_INC else 0):
                                for ci in range(2):
                                    cx = b * 2 + ci
                                    pp = pinc if cx % 2 == 0 else pinc2
                                    t.wait_ge(sems["ev"],
                                              ev0 + h * EVH + 18 + max(0, cx - 1))
                                    for sub in range(2):
                                        tt = b * 4 + ci * 2 + sub
                                        ins = t.matmul(
                                            pp[:, 0:128],
                                            ktm[:, tt * 128:(tt + 1) * 128],
                                            vtm2[:, (h % 2) * 2048 + tt * 128:
                                                 (h % 2) * 2048 + (tt + 1) * 128],
                                            start=(sub == 0), stop=(sub == 1))
                                    ins.then_inc(sems["aux"], 1)
                            if not DO_QP:
                                continue
                            t.wait_ge(sems["wld"], wld0 + h * 32 + 32)
                            t.wait_ge(sems["ev"], ev0 + h * EVH + 26)
                            if h >= 2 and DO_CHUNKS:
                                t.wait_ge(sems["ym"], ym0 + (h - 2) * 8 + 8)
                            for n in range(4):
                                t.wait_ge(sems["q"], q0 + h * 4 + n)
                                for dc in range(8):
                                    t.matmul(pq1[:, :],
                                             wq_sb[:, dc * 256:dc * 256 + 128],
                                             kb.fmbuf[:, dc * 2048 + n * 512:
                                                      dc * 2048 + (n + 1) * 512],
                                             start=(dc == 0), stop=(dc == 7))
                                for dc in range(8):
                                    ins = t.matmul(pq2[:, :],
                                                   wq_sb[:, dc * 256 + 128:
                                                         dc * 256 + 256],
                                                   kb.fmbuf[:, dc * 2048 + n * 512:
                                                            dc * 2048 + (n + 1) * 512],
                                                   start=(dc == 0), stop=(dc == 7))
                                ins.then_inc(sems["mm"], 1)
                            if not DO_CHUNKS:
                                continue
                            t.wait_ge(sems["ev"], ev0 + h * EVH + 18)
                            for ci in range(2):
                                for b in range(B):
                                    cx = ci * B + b
                                    gx = h * 8 + cx
                                    qc0 = (h % 2) * 2048 + b * 512 + ci * 256
                                    t.wait_ge(sems["q"], q0 + h * 4 + b + 1)
                                    t.wait_ge(sems["sc"], sc0 + gx)
                                    for ks in range(2):
                                        tt = b * 4 + ci * 2 + ks
                                        ins = t.matmul(
                                            psc[:, ks * 256:(ks + 1) * 256],
                                            kfm2[:, (h % 2) * 2048 + tt * 128:
                                                 (h % 2) * 2048 + (tt + 1) * 128],
                                            qfm[:, qc0:qc0 + 256],
                                            start=True, stop=True)
                                    ins.then_inc(sems["sm"], 1)
                                    t.wait_ge(sems["sc"], sc0 + gx + 1)
                                    if ci == 0:
                                        t.wait_ge(sems["tst"],
                                                  tst0 + 64 + h * 9 + 1)
                                    else:
                                        t.wait_ge(sems["tst"],
                                                  tst0 + 64 + h * 9 + 2 + b)
                                    if gx > 0:
                                        t.wait_ge(sems["tp"], tp0 + gx)
                                    for qs in range(2):
                                        pyap = py12[:, qs * 128:(qs + 1) * 128]
                                        t.matmul(pyap,
                                                 qfm[:, qc0 + qs * 128:
                                                     qc0 + qs * 128 + 128],
                                                 tstate_bf[:, b * 128:(b + 1) * 128],
                                                 start=True, stop=False)
                                        for ks in range(qs + 1):
                                            tt = b * 4 + ci * 2 + ks
                                            ins = t.matmul(
                                                pyap,
                                                sm_sb[:, ks * 256 + qs * 128:
                                                      ks * 256 + qs * 128 + 128],
                                                vtm2[:, (h % 2) * 2048 + tt * 128:
                                                     (h % 2) * 2048 + (tt + 1) * 128],
                                                start=False, stop=(ks == qs))
                                    ins.then_inc(sems["ym"], 1)

                    @blk.vector
                    def _(v):
                        if not DO_HEADS:
                            for hh in range(H):
                                for cp in range(NCORES):
                                    li = hh * NCORES + cp
                                    pslot = pfx[:, (li % 2) * 512:(li % 2 + 1) * 512]
                                    v.wait_ge(sems["st"], st0 + (li + 1) * 16)
                                    dst = tinit_all[:, hh * 512:(hh + 1) * 512]
                                    wv = wpref_sb[:, cp * 8 + hh:cp * 8 + hh + 1]
                                    if cp == 0:
                                        v.tensor_scalar(dst, pslot, wv, None,
                                                        ALU.mult).then_inc(sems["tst"], 1)
                                    else:
                                        v.scalar_tensor_tensor(dst, pslot, wv, dst,
                                                               ALU.mult,
                                                               ALU.add).then_inc(sems["tst"], 1)
                        for h in range(NH):
                            emit_kv_head_V(v, kb, h, pkv, pkv2, ktm,
                                           vtm2, (h % 2) * 2048,
                                           ckh_sb, skh_sb, ktmp1, ktmp2,
                                           mm0 + h * MMH, sems["ev"], sems["mm"])
                            for grp in range(2 if DO_TR else 0):
                                v.wait_ge(sems["mm"], mm0 + h * MMH + 16 + grp + 1)
                                v.tensor_copy(
                                    kfm2[:, (h % 2) * 2048 + grp * 1024:
                                         (h % 2) * 2048 + (grp + 1) * 1024],
                                    ptr1[:, :]).then_inc(sems["ev"], 1)
                            gC = float(GAMMAS[h] ** CH)
                            for b in range(B if DO_INC else 0):
                                for ci in range(2):
                                    cx = b * 2 + ci
                                    pp = pinc if cx % 2 == 0 else pinc2
                                    v.wait_ge(sems["aux"], aux0 + h * 8 + cx + 1)
                                    v.tensor_scalar(i2[:, cx * 128:(cx + 1) * 128],
                                                    pp[:, 0:128], gC, None,
                                                    ALU.mult).then_inc(sems["ev"], 1)
                            for n in range(4 if DO_QP else 0):
                                v.wait_ge(sems["mm"], mm0 + h * MMH + 18 + n + 1)
                                v.tensor_tensor(qtmp1[:, :], pq1[:, :],
                                                cqfm_sb[:, :], ALU.mult)
                                v.tensor_tensor(qtmp2[:, :], pq2[:, :],
                                                sqfm_sb[:, :], ALU.mult)
                                v.tensor_tensor(qfm[:, (h % 2) * 2048 + n * 512:
                                                    (h % 2) * 2048 + (n + 1) * 512],
                                                qtmp1[:, :], qtmp2[:, :],
                                                ALU.add).then_inc(sems["q"], 1)
                            if h == 0:
                                for hh in range(H):
                                    for cp in range(NCORES):
                                        li = hh * NCORES + cp
                                        pslot = pfx[:, (li % 2) * 512:
                                                    (li % 2 + 1) * 512]
                                        v.wait_ge(sems["st"], st0 + (li + 1) * 16)
                                        dst = tinit_all[:, hh * 512:(hh + 1) * 512]
                                        wv = wpref_sb[:, cp * 8 + hh:cp * 8 + hh + 1]
                                        if cp == 0:
                                            v.tensor_scalar(
                                                dst, pslot, wv, None,
                                                ALU.mult).then_inc(sems["tst"], 1)
                                        else:
                                            v.scalar_tensor_tensor(
                                                dst, pslot, wv, dst, ALU.mult,
                                                ALU.add).then_inc(sems["tst"], 1)
                            if not DO_CHUNKS:
                                continue
                            v.tensor_copy(tstate[:, :],
                                          tinit_all[:, h * 512:(h + 1) * 512])
                            v.tensor_copy(tstate_bf[:, :],
                                          tinit_all[:, h * 512:(h + 1) * 512]
                                          ).then_inc(sems["tst"], 1)
                            if h > 0:
                                v.wait_ge(sems["gp"], gp0 + h * 3)
                            for ci in range(2):
                                for b in range(B):
                                    cx = ci * B + b
                                    gx = h * 8 + cx
                                    v.wait_ge(sems["sm"], sm0 + gx + 1)
                                    v.tensor_tensor(sm_sb[:, 0:256],
                                                    psc[:, 0:256],
                                                    m0_sb[:, :], ALU.mult)
                                    v.tensor_tensor(sm_sb[:, 256:512],
                                                    psc[:, 256:512],
                                                    m1_sb[:, :],
                                                    ALU.mult).then_inc(sems["sc"], 1)
                                    v.wait_ge(sems["ym"], ym0 + gx + 1)
                                    for qs in range(2):
                                        pt = ci * 2 + qs
                                        ins = v.tensor_scalar(
                                            ystage[:, cx * 256 + qs * 128:
                                                   cx * 256 + (qs + 1) * 128],
                                            py12[:, qs * 128:(qs + 1) * 128],
                                            ysc_sb[:, h * 4 + pt:h * 4 + pt + 1],
                                            None, ALU.mult)
                                    ins.then_inc(sems["tp"], 1)
                                    gCh = float(GAMMAS[h] ** CH)
                                    col = (b * 2 + ci) * 128
                                    v.scalar_tensor_tensor(
                                        tstate[:, b * 128:(b + 1) * 128],
                                        tstate[:, b * 128:(b + 1) * 128], gCh,
                                        i2[:, col:col + 128], ALU.mult, ALU.add)
                                    v.tensor_copy(tstate_bf[:, b * 128:(b + 1) * 128],
                                                  tstate[:, b * 128:(b + 1) * 128]
                                                  ).then_inc(sems["tst"], 1)
                            v.wait_ge(sems["gp"], gp0 + h * 3 + 1)
                            v.tensor_reduce(gstat[:, 0:16],
                                            ystage[:, :].rearrange(
                                                "p (s v) -> p s v", s=16, v=128),
                                            AX.X, ALU.add)
                            v.tensor_reduce(gstat[:, 16:32],
                                            ysq[:, :].rearrange(
                                                "p (s v) -> p s v", s=16, v=128),
                                            AX.X, ALU.add)
                            v.drain()
                            v.tensor_scalar(gstat[:, 32:48], gstat[:, 0:16],
                                            1.0 / 128.0, None, ALU.mult)
                            v.tensor_scalar(gstat[:, 48:64], gstat[:, 16:32],
                                            1.0 / 128.0, None, ALU.mult)
                            v.drain()
                            v.tensor_tensor(gstat[:, 64:80], gstat[:, 32:48],
                                            gstat[:, 32:48], ALU.mult)
                            v.drain()
                            v.tensor_tensor(gstat[:, 48:64], gstat[:, 48:64],
                                            gstat[:, 64:80], ALU.subtract)
                            v.drain()
                            v.tensor_scalar(gstat[:, 48:64], gstat[:, 48:64],
                                            0.0, EPS, ALU.max, ALU.add)
                            v.drain()
                            v.reciprocal(gstat[:, 64:80],
                                         gstat[:, 48:64]).then_inc(sems["ev"], 1)
                            v.wait_ge(sems["gp"], gp0 + h * 3 + 2)
                            v.drain()
                            v.tensor_tensor(gstat[:, 96:112], gstat[:, 32:48],
                                            gstat[:, 80:96], ALU.mult)
                            v.drain()
                            v.tensor_scalar(gstat[:, 96:112], gstat[:, 96:112],
                                            -1.0, None,
                                            ALU.mult).then_inc(sems["ev"], 1)

                    @blk.scalar
                    def _(s):
                        for h in range(NH if DO_CHUNKS else 0):
                            s.wait_ge(sems["tp"], tp0 + h * 8 + 8)
                            s.activation(ysq[:, :], ystage[:, :],
                                         ACT.Square).then_inc(sems["gp"], 1)
                            s.wait_ge(sems["ev"], ev0 + h * 28 + 27)
                            s.drain()
                            s.activation(gstat[:, 80:96], gstat[:, 64:80],
                                         ACT.Sqrt).then_inc(sems["gp"], 1)
                            s.wait_ge(sems["ev"], ev0 + h * 28 + 28)
                            s.drain()
                            for ci in range(2):
                                for b in range(B):
                                    cx = ci * B + b
                                    for qs in range(2):
                                        sub = cx * 2 + qs
                                        tt = b * 4 + ci * 2 + qs
                                        ins = s.activation(
                                            yn[:, tt * 1024 + h * 128:
                                               tt * 1024 + (h + 1) * 128],
                                            ystage[:, sub * 128:(sub + 1) * 128],
                                            ACT.Identity,
                                            bias=gstat[:, 96 + sub:97 + sub],
                                            scale=gstat[:, 80 + sub:81 + sub])
                            ins.then_inc(sems["gp"], 1)

        cnt["cc"] += 1
        cnt["st"] += 64 * 16
        cnt["tst"] += 64 + (NH * 9 if DO_CHUNKS else 0)
        cnt["cst"] += 106 * 16
        cnt["wld"] += NH * 32
        cnt["mm"] += NH * MMH
        cnt["ev"] += NH * EVH
        cnt["aux"] += NH * 8 if DO_INC else 0
        cnt["q"] += NH * 4 if DO_QP else 0
        if DO_CHUNKS:
            cnt["sm"] += NH * 8
            cnt["ym"] += NH * 8
            cnt["sc"] += NH * 8
            cnt["tp"] += NH * 8
            cnt["gp"] += NH * 3
        if SUBSTOP.startswith("A2"):
            return
        if DBG:
            dby = nc.dram_tensor("dbg_yn", [128, NT * 1024], F32,
                                 kind="ExternalOutput").ap()
            dbt = nc.dram_tensor("dbg_tinit", [128, H * 512], F32,
                                 kind="ExternalOutput").ap()
            dbk = nc.dram_tensor("dbg_ktm", [128, 2048], F32,
                                 kind="ExternalOutput").ap()
            dbq = nc.dram_tensor("dbg_qfm", [128, 4096], F32,
                                 kind="ExternalOutput").ap()
            dbv = nc.dram_tensor("dbg_vtm", [128, 4096], F32,
                                 kind="ExternalOutput").ap()
            dbkf = nc.dram_tensor("dbg_kfm", [128, 4096], F32,
                                  kind="ExternalOutput").ap()
            dbi = nc.dram_tensor("dbg_i2", [128, 1024], F32,
                                 kind="ExternalOutput").ap()
            dbsm = nc.dram_tensor("dbg_sm", [128, 512], F32,
                                  kind="ExternalOutput").ap()
            dbys = nc.dram_tensor("dbg_ysb", [128, 256], F32,
                                  kind="ExternalOutput").ap()
            dbgl = nc.dram_tensor("dbg_gl", [128, 16], F32,
                                  kind="ExternalOutput").ap()
            dbts = nc.dram_tensor("dbg_ts", [128, 512], F32,
                                  kind="ExternalOutput").ap()
            with nc.Block("dbg_a2") as blk:
                @blk.gpsimd
                def _(g):
                    for dst, srcb in [(dby, yn[:, :]), (dbt, tinit_all[:, :]),
                                      (dbk, ktm[:, :]), (dbq, qfm[:, :]),
                                      (dbv, vtm2[:, :]), (dbkf, kfm2[:, :]),
                                      (dbi, i2[:, :]), (dbsm, sm_sb[:, :]),
                                      (dbys, ysb[:, :]), (dbgl, gl[:, :]),
                                      (dbts, tstate_bf[:, :])]:
                        g.dma_start(out=dst, in_=srcb).then_inc(sems["st"], 16)
                    g.wait_ge(sems["st"], cnt["st"] + 11 * 16)
            cnt["st"] += 11 * 16

        mm0, ev0, gp0, aux0, wld0 = (cnt["mm"], cnt["ev"], cnt["gp"],
                                     cnt["aux"], cnt["wld"])
        with ExitStack() as a3s:
            wg_sb = a3s.enter_context(nc.sbuf_tensor("wg_sb", [128, 8192], BF16))
            wo_sb = a3s.enter_context(nc.sbuf_tensor("wo_sb", [128, 8192], BF16))
            gsil2 = a3s.enter_context(nc.sbuf_tensor("gsil2", [128, 2048], BF16))
            tnorm2 = a3s.enter_context(nc.sbuf_tensor("tnorm2", [128, 2048], BF16))
            mfm2 = a3s.enter_context(nc.sbuf_tensor("mfm2", [128, 2048], BF16))
            with (nc.psum_tensor("pg", [128, 1024], F32) as pg,
                  nc.psum_tensor("pmt", [128, 1024], BF16) as pmt,
                  nc.psum_tensor("po_a", [128, 1024], F32) as po_a,
                  nc.psum_tensor("po_b", [128, 1024], F32) as po_b):
                qa2 = cnt["cst"]
                with nc.Block("A3") as blk:
                    @blk.gpsimd
                    def _(g):
                        for m in range(32):
                            g.dma_start(out=T["h2t_bf"][m],
                                        in_=T["h2t"][m]).then_inc(sems["cst"], 16)

                    @blk.sync
                    def _(s):
                        s.wait_ge(sems["cst"], qa2 - 104 * 16)
                        s.dma_start(out=wg_sb[:, :], in_=T["wg_bf"]).then_inc(sems["wld"], 16)
                        s.dma_start(out=wo_sb[:, :], in_=T["wo_bf"]).then_inc(sems["wld"], 16)

                    @blk.tensor
                    def _(t):
                        t.wait_ge(sems["wld"], wld0 + 32)
                        for tt in range(NT):
                            sl = tt % 2
                            if tt > 0:
                                t.wait_ge(sems["gp"], gp0 + 4 * (tt - 1) + 1)
                            for half in range(2):
                                for dc in range(8):
                                    ins = t.matmul(
                                        pg[:, half * 512:(half + 1) * 512],
                                        kb.fmbuf[:, dc * 2048 + tt * 128:
                                                 dc * 2048 + tt * 128 + 128],
                                        wg_sb[:, dc * 1024 + half * 512:
                                              dc * 1024 + (half + 1) * 512],
                                        start=(dc == 0), stop=(dc == 7))
                            ins.then_inc(sems["mm"], 1)
                            t.wait_ge(sems["ev"], ev0 + tt * 3 + 2)
                            if tt > 0:
                                t.wait_ge(sems["ev"], ev0 + (tt - 1) * 3 + 3)
                            for dc in range(8):
                                ins = t.transpose(pmt[:, dc * 128:(dc + 1) * 128],
                                                  tnorm2[:, sl * 1024 + dc * 128:
                                                         sl * 1024 + (dc + 1) * 128],
                                                  kb.idb[:, :])
                            ins.then_inc(sems["mm"], 1)
                            t.wait_ge(sems["ev"], ev0 + tt * 3 + 3)
                            if tt >= 2:
                                t.wait_ge(sems["aux"], aux0 + (tt - 2) * 3 + 1)
                            po = po_a if sl == 0 else po_b
                            for half in range(2):
                                for dc in range(8):
                                    ins = t.matmul(
                                        po[:, half * 512:(half + 1) * 512],
                                        mfm2[:, sl * 1024 + dc * 128:
                                             sl * 1024 + (dc + 1) * 128],
                                        wo_sb[:, dc * 1024 + half * 512:
                                              dc * 1024 + (half + 1) * 512],
                                        start=(dc == 0), stop=(dc == 7))
                            ins.then_inc(sems["mm"], 1)

                    @blk.scalar
                    def _(s):
                        for tt in range(NT):
                            sl = tt % 2
                            s.wait_ge(sems["mm"], mm0 + tt * 3 + 1)
                            if tt >= 2:
                                s.wait_ge(sems["ev"], ev0 + (tt - 2) * 3 + 2)
                            s.activation(gsil2[:, sl * 1024:(sl + 1) * 1024],
                                         pg[:, :], ACT.Silu).then_inc(sems["gp"], 1)
                            s.wait_ge(sems["aux"], aux0 + tt * 3 + 1)
                            ln_s1(s, kb, kb.r32[:, sl * 1024:(sl + 1) * 1024],
                                  sl).then_inc(sems["gp"], 1)
                            s.wait_ge(sems["aux"], aux0 + tt * 3 + 2)
                            ln_s2_sqrt(s, kb, sl).then_inc(sems["gp"], 1)
                            s.wait_ge(sems["aux"], aux0 + tt * 3 + 3)
                            ln_apply(s, kb, kb.r32[:, sl * 1024:(sl + 1) * 1024],
                                     kb.stream[:, tt * 1024:(tt + 1) * 1024],
                                     sl).then_inc(sems["gp"], 1)

                    @blk.vector
                    def _(v):
                        for tt in range(NT):
                            sl = tt % 2
                            if tt >= 2:
                                v.wait_ge(sems["mm"], mm0 + 3 * (tt - 2) + 2)
                            v.tensor_tensor(tnorm2[:, sl * 1024:(sl + 1) * 1024],
                                            yn[:, tt * 1024:(tt + 1) * 1024],
                                            gnw_sb[:, :], ALU.mult)
                            v.drain()
                            v.tensor_tensor(tnorm2[:, sl * 1024:(sl + 1) * 1024],
                                            tnorm2[:, sl * 1024:(sl + 1) * 1024],
                                            gnb_sb[:, :],
                                            ALU.add).then_inc(sems["ev"], 1)
                            v.wait_ge(sems["gp"], gp0 + tt * 4 + 1)
                            v.tensor_tensor(tnorm2[:, sl * 1024:(sl + 1) * 1024],
                                            gsil2[:, sl * 1024:(sl + 1) * 1024],
                                            tnorm2[:, sl * 1024:(sl + 1) * 1024],
                                            ALU.mult).then_inc(sems["ev"], 1)
                            v.wait_ge(sems["mm"], mm0 + tt * 3 + 2)
                            v.tensor_copy(mfm2[:, sl * 1024:(sl + 1) * 1024],
                                          pmt[:, :]).then_inc(sems["ev"], 1)
                            v.wait_ge(sems["mm"], mm0 + tt * 3 + 3)
                            if tt >= 2:
                                v.wait_ge(sems["gp"], gp0 + (tt - 2) * 4 + 4)
                            po = po_a if sl == 0 else po_b
                            v.tensor_tensor(kb.r32[:, sl * 1024:(sl + 1) * 1024],
                                            po[:, :],
                                            kb.stream[:, tt * 1024:(tt + 1) * 1024],
                                            ALU.add)
                            ln_v1(v, kb, kb.r32[:, sl * 1024:(sl + 1) * 1024],
                                  sl).then_inc(sems["aux"], 1)
                            v.wait_ge(sems["gp"], gp0 + tt * 4 + 2)
                            ln_v2(v, kb, sl).then_inc(sems["aux"], 1)
                            v.wait_ge(sems["gp"], gp0 + tt * 4 + 3)
                            ln_v3(v, kb, sl).then_inc(sems["aux"], 1)

        cnt["mm"] += NT * 3
        cnt["ev"] += NT * 3
        cnt["gp"] += NT * 4
        cnt["aux"] += NT * 3
        cnt["wld"] += 32
        cnt["cst"] += 32 * 16


def pp_slice(ptr, tt):
    return ptr[:, tt * 128:(tt + 1) * 128]







def emit_phase_b(kb):
    nc, sems, cnt, T = kb.nc, kb.sems, kb.cnt, kb.T

    emit_tm_to_fm(kb, "B0")

    with ExitStack() as bes:
        sb = lambda name, shp, dt=BF16: bes.enter_context(nc.sbuf_tensor(name, shp, dt))
        X1 = sb("X1", [128, 32 * 512])
        X2 = sb("X2", [128, 32 * 512])
        w12 = sb("w12", [128, 4 * 1024])
        wh = sb("wh", [128, 2 * 4096])
        w1s_sb = sb("w1s_sb", [128, 32], F32)
        w2s_sb = sb("w2s_sb", [128, 32], F32)
        h1s_sb = sb("h1s_sb", [128, 32], F32)
        h2s_sb = sb("h2s_sb", [128, 32], F32)
        w3s_sb = sb("w3s_sb", [128, 8], F32)
        t1s = sb("t1s", [128, 512])
        t2s = sb("t2s", [128, 512])
        ofm = sb("ofm", [128, 8 * 512])

        ld0 = cnt["ld"]
        with nc.Block("Bld") as blk:
            @blk.sync
            def _(s):
                for ap, src in [(w1s_sb, "w1s"), (w2s_sb, "w2s"), (h1s_sb, "h1s"),
                                (h2s_sb, "h2s"), (w3s_sb, "w3s")]:
                    s.dma_start(out=ap[:, :], in_=T[src]).then_inc(sems["ld"], 16)
                s.wait_ge(sems["ld"], ld0 + 5 * 16)
        cnt["ld"] += 5 * 16

        MMTC = 108
        GPTC = 108
        EVTC = 44
        WLTC = 136 * 16
        mm0, gp0, ev0, wld0, aux0 = (cnt["mm"], cnt["gp"], cnt["ev"], cnt["wld"],
                                     cnt["aux"])

        def tc_base(tc):
            return (mm0 + tc * MMTC, gp0 + tc * GPTC, ev0 + tc * EVTC,
                    wld0 + tc * WLTC, aux0 + tc * 8)

        with (nc.psum_tensor("p1a", [128, 512], F32) as p1a,
              nc.psum_tensor("p2a", [128, 512], F32) as p2a,
              nc.psum_tensor("p1b", [128, 512], F32) as p1b,
              nc.psum_tensor("p2b", [128, 512], F32) as p2b,
              nc.psum_tensor("pga", [128, 512], F32) as pga,
              nc.psum_tensor("pgb", [128, 512], F32) as pgb,
              nc.psum_tensor("pmt_b", [128, 1024], BF16) as pmt):
            q_all = cnt["cst"]
            with nc.Block("B1") as blk:
                @blk.sync
                def _(g):
                    g.wait_ge(sems["cst"], q_all)
                    for tc in range(4):
                        mmb, gpb, evb, wlb, auxb = tc_base(tc)
                        for m in range(32):
                            g.wait_ge(sems["mm"], mmb + m - 1 if m >= 2 else mmb)
                            g.dma_start(out=w12[:, (m % 2) * 2048:(m % 2) * 2048 + 1024],
                                        in_=T["w1t_bf"][m]).then_inc(sems["wld"], 16)
                            g.dma_start(out=w12[:, (m % 2) * 2048 + 1024:
                                               (m % 2) * 2048 + 2048],
                                        in_=T["w2t_bf"][m]).then_inc(sems["wld"], 16)
                        for st, wt, nm, moff in [(2, "h1t_bf", 32, 32),
                                                 (3, "h2t_bf", 32, 64),
                                                 (4, "w3t_bf", 8, 96)]:
                            for m in range(nm):
                                if st == 2 and m < 2:
                                    g.wait_ge(sems["mm"], mmb)
                                else:
                                    g.wait_ge(sems["mm"], mmb + moff + m - 1)
                                g.dma_start(out=wh[:, (m % 2) * 4096:
                                                   (m % 2 + 1) * 4096],
                                            in_=T[wt][m]).then_inc(sems["wld"], 16)

                @blk.tensor
                def _(t):
                    for tc in range(4):
                        mmb, gpb, evb, wlb, auxb = tc_base(tc)
                        for m in range(32):
                            t.wait_ge(sems["wld"], wlb + (m + 1) * 32)
                            if m >= 2:
                                t.wait_ge(sems["gp"], gpb + m - 1)
                                t.wait_ge(sems["ev"], evb + m - 1)
                            p1 = p1a if m % 2 == 0 else p1b
                            p2 = p2a if m % 2 == 0 else p2b
                            for dc in range(8):
                                t.matmul(p1[:, :],
                                         w12[:, (m % 2) * 2048 + dc * 128:
                                             (m % 2) * 2048 + (dc + 1) * 128],
                                         kb.fmbuf[:, dc * 2048 + tc * 512:
                                                  dc * 2048 + (tc + 1) * 512],
                                         start=(dc == 0), stop=(dc == 7))
                            for dc in range(8):
                                ins = t.matmul(p2[:, :],
                                               w12[:, (m % 2) * 2048 + 1024 + dc * 128:
                                                   (m % 2) * 2048 + 1024 + (dc + 1) * 128],
                                               kb.fmbuf[:, dc * 2048 + tc * 512:
                                                        dc * 2048 + (tc + 1) * 512],
                                               start=(dc == 0), stop=(dc == 7))
                            ins.then_inc(sems["mm"], 1)
                        for st, nm, moff, src in [(2, 32, 32, X1), (3, 32, 64, X2),
                                                  (4, 8, 96, X1)]:
                            if st == 2:
                                t.wait_ge(sems["ev"], evb + 32)
                            elif st == 3:
                                t.wait_ge(sems["gp"], gpb + 64)
                            else:
                                t.wait_ge(sems["gp"], gpb + 96)
                            for m in range(nm):
                                t.wait_ge(sems["wld"], wlb + 64 * 16 +
                                          (moff - 32 + m + 1) * 16)
                                if m >= 2:
                                    if st == 4:
                                        t.wait_ge(sems["ev"], evb + 32 + m - 1)
                                    else:
                                        t.wait_ge(sems["gp"], gpb + moff + m - 1)
                                pg = pga if m % 2 == 0 else pgb
                                for k in range(32):
                                    ins = t.matmul(pg[:, :],
                                                   wh[:, (m % 2) * 4096 + k * 128:
                                                      (m % 2) * 4096 + (k + 1) * 128],
                                                   src[:, k * 512:(k + 1) * 512],
                                                   start=(k == 0), stop=(k == 31))
                                ins.then_inc(sems["mm"], 1)
                        t.wait_ge(sems["ev"], evb + 40)
                        for tl in range(4):
                            tt = tc * 4 + tl
                            if tl > 0:
                                t.wait_ge(sems["ev"], evb + 40 + tl)
                            for m in range(8):
                                ins = t.transpose(pmt[:, m * 128:(m + 1) * 128],
                                                  ofm[:, m * 512 + tl * 128:
                                                      m * 512 + (tl + 1) * 128],
                                                  kb.idb[:, :])
                            ins.then_inc(sems["mm"], 1)

                @blk.scalar
                def _(s):
                    for tc in range(4):
                        mmb, gpb, evb, wlb, auxb = tc_base(tc)
                        for m in range(32):
                            s.wait_ge(sems["mm"], mmb + m + 1)
                            if m >= 1:
                                s.wait_ge(sems["ev"], evb + m)
                            elif tc > 0:
                                s.wait_ge(sems["ev"], evb - EVTC + 32)
                            p1 = p1a if m % 2 == 0 else p1b
                            s.activation(t1s[:, :], p1[:, :], ACT.Silu,
                                         scale=w1s_sb[:, m:m + 1]
                                         ).then_inc(sems["gp"], 1)
                        for st, nm, moff, dst, ssb in [(2, 32, 32, X2, h1s_sb),
                                                       (3, 32, 64, X1, h2s_sb)]:
                            for m in range(nm):
                                s.wait_ge(sems["mm"], mmb + moff + m + 1)
                                pg = pga if m % 2 == 0 else pgb
                                s.activation(dst[:, m * 512:(m + 1) * 512], pg[:, :],
                                             ACT.Gelu, scale=ssb[:, m:m + 1]
                                             ).then_inc(sems["gp"], 1)
                        for tl in range(4):
                            tt = tc * 4 + tl
                            s.wait_ge(sems["aux"], auxb + tl * 2 + 1)
                            ln_s1(s, kb, kb.r32[:, 0:1024]).then_inc(sems["gp"], 1)
                            s.wait_ge(sems["aux"], auxb + tl * 2 + 2)
                            ln_s2_sqrt(s, kb).then_inc(sems["gp"], 1)
                            s.wait_ge(sems["ev"], evb + 41 + tl)
                            ln_apply(s, kb, kb.r32[:, 0:1024],
                                     kb.stream[:, tt * 1024:(tt + 1) * 1024]
                                     ).then_inc(sems["gp"], 1)

                @blk.vector
                def _(v):
                    for tc in range(4):
                        mmb, gpb, evb, wlb, auxb = tc_base(tc)
                        for m in range(32):
                            v.wait_ge(sems["mm"], mmb + m + 1)
                            v.wait_ge(sems["gp"], gpb + m + 1)
                            p2 = p2a if m % 2 == 0 else p2b
                            v.tensor_scalar(t2s[:, :], p2[:, :],
                                            w2s_sb[:, m:m + 1], None, ALU.mult)
                            v.tensor_tensor(X1[:, m * 512:(m + 1) * 512],
                                            t1s[:, :], t2s[:, :],
                                            ALU.mult).then_inc(sems["ev"], 1)
                        for m in range(8):
                            v.wait_ge(sems["mm"], mmb + 96 + m + 1)
                            pg = pga if m % 2 == 0 else pgb
                            v.tensor_scalar(ofm[:, m * 512:(m + 1) * 512], pg[:, :],
                                            w3s_sb[:, m:m + 1], None,
                                            ALU.mult).then_inc(sems["ev"], 1)
                        for tl in range(4):
                            tt = tc * 4 + tl
                            v.wait_ge(sems["mm"], mmb + 104 + tl + 1)
                            if tl > 0:
                                v.wait_ge(sems["gp"], gpb + 96 + 3 * tl)
                            elif tc > 0:
                                v.wait_ge(sems["gp"], gpb)
                            v.tensor_tensor(kb.r32[:, 0:1024], pmt[:, :],
                                            kb.stream[:, tt * 1024:(tt + 1) * 1024],
                                            ALU.add)
                            ln_v1(v, kb, kb.r32[:, 0:1024]).then_inc(sems["aux"], 1)
                            v.wait_ge(sems["gp"], gpb + 96 + tl * 3 + 1)
                            ln_v2(v, kb).then_inc(sems["aux"], 1)
                            v.wait_ge(sems["gp"], gpb + 96 + tl * 3 + 2)
                            ln_v3(v, kb).then_inc(sems["ev"], 1)

        cnt["mm"] += 4 * MMTC
        cnt["gp"] += 4 * GPTC
        cnt["ev"] += 4 * EVTC
        cnt["wld"] += 4 * WLTC
        cnt["aux"] += 4 * 8

def emit_phase_c(kb):
    nc, sems, cnt, T = kb.nc, kb.sems, kb.cnt, kb.T

    emit_tm_to_fm(kb, "C0")

    wld0 = cnt["wld"]
    kb.epsb = kb.es.enter_context(nc.sbuf_tensor("epsb", [128, 1], F32))
    with nc.Block("Cpre") as blk:
        @blk.gpsimd
        def _(g):
            g.memset(kb.epsb[:, :], 1e-6)
            g.dma_start(out=T["wkvq2_bf"], in_=T["wkvq2"]).then_inc(sems["wld"], 16)
            g.wait_ge(sems["wld"], wld0 + 16)
    cnt["wld"] += 16

    with ExitStack() as ces:
        sb = lambda name, shp, dt=F32: ces.enter_context(nc.sbuf_tensor(name, shp, dt))
        Ssb = sb("Ssb", [128, 8192])
        scr = sb("scr", [128, 8192])
        ksb = sb("ksb", [128, 1024]); vsb = sb("vsb", [128, 1024])
        qsb = sb("qsb", [128, 1024]); knsb = sb("knsb", [128, 1024])
        dsb = sb("dsb", [128, 1024])
        sqscr = sb("sqscr", [128, 1024])
        Sqb = sb("Sqb", [128, 1024])
        scr2 = sb("scr2", [128, 1024])
        silscr = sb("silscr", [128, 1024])
        ssv = sb("ssv", [128, 128]); ssr = sb("ssr", [128, 128])
        invn = sb("invn", [128, 128])
        wslc = ces.enter_context(nc.sbuf_tensor("wslc", [128, 2 * 3072], BF16))

        mm0, ev0, gp0, sc0, tp0, aux0, wld0, ld0 = (
            cnt["mm"], cnt["ev"], cnt["gp"], cnt["sc"], cnt["tp"], cnt["aux"],
            cnt["wld"], cnt["ld"])

        def sc_ix(it, step):
            qt, t = it // 4, it % 4
            return sc0 + qt * 21 + 1 + t * 5 + step

        with (nc.psum_tensor("pk_c", [128, 1024], F32) as pk,
              nc.psum_tensor("pv_c", [128, 1024], F32) as pv,
              nc.psum_tensor("pq_c", [128, 1024], F32) as pq):
            with nc.Block("C1") as blk:
                @blk.sync
                def _(s):
                    s.wait_ge(sems["wld"], wld0)
                    for it in range(16):
                        for dc in range(8):
                            if it * 8 + dc >= 2:
                                s.wait_ge(sems["mm"], mm0 + it * 8 + dc - 1)
                            s.dma_start(out=wslc[:, (dc % 2) * 3072:(dc % 2 + 1) * 3072],
                                        in_=T["wkvq2_bf"][:, dc * 3072:(dc + 1) * 3072]
                                        ).then_inc(sems["ld"], 16)

                @blk.tensor
                def _(t_):
                    for it in range(16):
                        qt, t = it // 4, it % 4
                        tile = t * 4 + qt
                        if it >= 1:
                            t_.wait_ge(sems["ev"], ev0 + (it - 1) * 4 + 1)
                            t_.wait_ge(sems["gp"], gp0 + (it - 1) * 7 + 1)
                        for dc in range(8):
                            t_.wait_ge(sems["ld"], ld0 + (it * 8 + dc + 1) * 16)
                            lhs = kb.fmbuf[:, dc * 2048 + tile * 128:
                                           dc * 2048 + (tile + 1) * 128]
                            for wi, pp in [(0, pk), (1, pv), (2, pq)]:
                                for half in range(2):
                                    ins = t_.matmul(
                                        pp[:, half * 512:(half + 1) * 512],
                                        lhs,
                                        wslc[:, (dc % 2) * 3072 + wi * 1024 +
                                             half * 512:(dc % 2) * 3072 + wi * 1024 +
                                             (half + 1) * 512],
                                        start=(dc == 0), stop=(dc == 7))
                            ins.then_inc(sems["mm"], 1)

                @blk.vector
                def _(v):
                    for it in range(16):
                        qt, t = it // 4, it % 4
                        tile = t * 4 + qt
                        if t == 0:
                            if qt > 0:
                                v.wait_ge(sems["tp"], tp0 + qt * 4)
                                v.wait_ge(sems["aux"], aux0 + qt * 12)
                            v.memset(Ssb[:, :], 0.0).then_inc(sems["sc"], 1)
                        v.wait_ge(sems["mm"], mm0 + it * 8 + 8)
                        v.tensor_copy(ksb[:, :], pk[:, :]).then_inc(sems["ev"], 1)
                        v.wait_ge(sems["gp"], gp0 + it * 7 + 2)
                        v.tensor_reduce(ssv[:, :],
                                        sqscr[:, :].rearrange("p (nb j) -> p nb j",
                                                              nb=128, j=8),
                                        AX.X, ALU.add).then_inc(sems["sc"], 1)
                        v.wait_ge(sems["gp"], gp0 + it * 7 + 3)
                        v.reciprocal(invn[:, :], ssr[:, :])
                        v.drain()
                        kn_in1 = invn[:, :].rearrange("p nb -> p nb", nb=128
                                                      ) if False else \
                            invn[:, :].unsqueeze(2).broadcast_to([128, 128, 8])
                        v.tensor_tensor(knsb[:, :].rearrange("p (nb j) -> p nb j",
                                                             nb=128, j=8),
                                        ksb[:, :].rearrange("p (nb j) -> p nb j",
                                                            nb=128, j=8),
                                        kn_in1, ALU.mult).then_inc(sems["sc"], 1)
                        v.wait_ge(sems["aux"], aux0 + it * 3 + 1)
                        v.tensor_reduce(dsb[:, :].rearrange("p (nb i) -> p nb i",
                                                            nb=128, i=8),
                                        scr[:, :].rearrange("p (nb i j) -> p nb i j",
                                                            nb=128, i=8, j=8),
                                        AX.X, ALU.add)
                        v.tensor_tensor(dsb[:, :], vsb[:, :], dsb[:, :],
                                        ALU.subtract).then_inc(sems["sc"], 1)
                        v.wait_ge(sems["aux"], aux0 + it * 3 + 2)
                        v.tensor_tensor(scr[:, :], Ssb[:, :], scr[:, :],
                                        ALU.add).then_inc(sems["sc"], 1)
                        v.wait_ge(sems["aux"], aux0 + it * 3 + 3)
                        v.tensor_reduce(Sqb[:, :].rearrange("p (nb i) -> p nb i",
                                                            nb=128, i=8),
                                        scr[:, :].rearrange("p (nb i j) -> p nb i j",
                                                            nb=128, i=8, j=8),
                                        AX.X, ALU.add).then_inc(sems["sc"], 1)
                        v.wait_ge(sems["gp"], gp0 + it * 7 + 4)
                        v.tensor_tensor(scr2[:, :], Sqb[:, :], silscr[:, :], ALU.mult)
                        if it >= 1:
                            v.wait_ge(sems["gp"], gp0 + (it - 1) * 7 + 7)
                        v.tensor_tensor(kb.r32[:, 0:1024], scr2[:, :],
                                        kb.stream[:, tile * 1024:(tile + 1) * 1024],
                                        ALU.add)
                        ln_v1(v, kb, kb.r32[:, 0:1024]).then_inc(sems["ev"], 1)
                        v.wait_ge(sems["gp"], gp0 + it * 7 + 5)
                        ln_v2(v, kb).then_inc(sems["ev"], 1)
                        v.wait_ge(sems["gp"], gp0 + it * 7 + 6)
                        ln_v3(v, kb).then_inc(sems["ev"], 1)

                @blk.scalar
                def _(s):
                    for it in range(16):
                        qt, t = it // 4, it % 4
                        tile = t * 4 + qt
                        s.wait_ge(sems["mm"], mm0 + it * 8 + 8)
                        if it >= 1:
                            s.wait_ge(sems["sc"], sc_ix(it - 1, 5))
                        s.copy(vsb[:, :], pv[:, :])
                        s.copy(qsb[:, :], pq[:, :]).then_inc(sems["gp"], 1)
                        s.wait_ge(sems["ev"], ev0 + it * 4 + 1)
                        s.activation(sqscr[:, :], ksb[:, :],
                                     ACT.Square).then_inc(sems["gp"], 1)
                        s.wait_ge(sems["sc"], sc_ix(it, 1))
                        s.activation(ssr[:, :], ssv[:, :], ACT.Sqrt)
                        s.drain()
                        s.activation(ssr[:, :], ssr[:, :], ACT.Identity,
                                     bias=kb.epsb[:, 0:1]).then_inc(sems["gp"], 1)
                        s.wait_ge(sems["sc"], sc_ix(it, 4))
                        s.activation(Ssb[:, :], scr[:, :],
                                     ACT.Tanh).then_inc(sems["tp"], 1)
                        s.wait_ge(sems["sc"], sc_ix(it, 5))
                        s.activation(silscr[:, :], Sqb[:, :],
                                     ACT.Silu).then_inc(sems["gp"], 1)
                        s.wait_ge(sems["ev"], ev0 + it * 4 + 2)
                        ln_s1(s, kb, kb.r32[:, 0:1024]).then_inc(sems["gp"], 1)
                        s.wait_ge(sems["ev"], ev0 + it * 4 + 3)
                        ln_s2_sqrt(s, kb).then_inc(sems["gp"], 1)
                        s.wait_ge(sems["ev"], ev0 + it * 4 + 4)
                        ln_apply(s, kb, kb.r32[:, 0:1024],
                                 kb.stream[:, tile * 1024:(tile + 1) * 1024]
                                 ).then_inc(sems["gp"], 1)

                @blk.gpsimd
                def _(g):
                    for it in range(16):
                        g.wait_ge(sems["sc"], sc_ix(it, 2))
                        g.tensor_tensor(scr[:, :].rearrange("p (nb i j) -> p nb i j",
                                                            nb=128, i=8, j=8),
                                        Ssb[:, :].rearrange("p (nb i j) -> p nb i j",
                                                            nb=128, i=8, j=8),
                                        knsb[:, :].rearrange("p (nb j) -> p nb j",
                                                             nb=128, j=8)
                                        .unsqueeze(2).broadcast_to([128, 128, 8, 8]),
                                        ALU.mult).then_inc(sems["aux"], 1)
                        g.wait_ge(sems["sc"], sc_ix(it, 3))
                        g.tensor_tensor(scr[:, :].rearrange("p (nb i j) -> p nb i j",
                                                            nb=128, i=8, j=8),
                                        dsb[:, :].rearrange("p (nb i) -> p nb i",
                                                            nb=128, i=8)
                                        .unsqueeze(3).broadcast_to([128, 128, 8, 8]),
                                        knsb[:, :].rearrange("p (nb j) -> p nb j",
                                                             nb=128, j=8)
                                        .unsqueeze(2).broadcast_to([128, 128, 8, 8]),
                                        ALU.mult).then_inc(sems["aux"], 1)
                        g.wait_ge(sems["tp"], tp0 + it + 1)
                        g.tensor_tensor(scr[:, :].rearrange("p (nb i j) -> p nb i j",
                                                            nb=128, i=8, j=8),
                                        Ssb[:, :].rearrange("p (nb i j) -> p nb i j",
                                                            nb=128, i=8, j=8),
                                        qsb[:, :].rearrange("p (nb j) -> p nb j",
                                                            nb=128, j=8)
                                        .unsqueeze(2).broadcast_to([128, 128, 8, 8]),
                                        ALU.mult).then_inc(sems["aux"], 1)

        cnt["mm"] += 16 * 8
        cnt["ev"] += 16 * 4
        cnt["gp"] += 16 * 7
        cnt["sc"] += 4 * 21
        cnt["tp"] += 16
        cnt["aux"] += 16 * 3
        cnt["ld"] += 16 * 8 * 16

def emit_phase_d(kb):
    nc, sems, cnt, T = kb.nc, kb.sems, kb.cnt, kb.T

    emit_tm_to_fm(kb, "D0")

    with ExitStack() as des:
        sb = lambda name, shp, dt=BF16: des.enter_context(nc.sbuf_tensor(name, shp, dt))
        wr_sb = sb("wr_sb", [128, 8192])
        wk_sb = sb("wk_sb", [128, 8192])
        wv_sb = sb("wv_sb", [128, 8192])
        mur_sb = sb("mur_sb", [128, 8], F32)
        muk_sb = sb("muk_sb", [128, 8], F32)
        hflag = sb("hflag", [128, 8], F32)
        halocand = sb("halocand", [128, 256])
        cc_sb = sb("cc_sb", [32, 1024])
        halo_fm = sb("halo_fm", [128, 32], F32)
        xp = sb("xp", [128, 4096])
        xr = sb("xr", [128, 4096])
        xk = sb("xk", [128, 4096])
        dseg = sb("dseg", [128, 512], F32)
        rbuf = sb("rbuf", [128, 4096])
        krelu = sb("krelu", [128, 4096])
        kact = sb("kact", [128, 4096])
        out_fm = sb("out_fm", [128, 4096])
        yout = sb("yout", [128, 2 * 1024], F32)

        wld0, st0, cc0, tp0, ld0 = (cnt["wld"], cnt["st"], cnt["cc"], cnt["tp"],
                                    cnt["ld"])
        mmpre = cnt["mm"]
        phl = des.enter_context(nc.psum_tensor("phl", [128, 256], BF16))
        with nc.Block("Dpre") as blk:
            @blk.sync
            def _(s):
                for ap, src in [(mur_sb, "mur"), (muk_sb, "muk"), (hflag, "haloflag")]:
                    s.dma_start(out=ap[:, :], in_=T[src]).then_inc(sems["ld"], 16)
                s.wait_ge(sems["ld"], ld0 + 3 * 16)

            @blk.gpsimd
            def _(g):
                for ap, src in [(wr_sb, "wr"), (wk_sb, "wkd"), (wv_sb, "wvd")]:
                    g.dma_start(out=ap[:, :], in_=T[src]).then_inc(sems["wld"], 16)
                for b in range(B):
                    tile = b * 4 + 3
                    g.dma_start(out=T["cc_h_in"][b:b + 1, :],
                                in_=kb.stream[127:128, tile * 1024:(tile + 1) * 1024]
                                ).then_inc(sems["wld"], 16)
                g.wait_ge(sems["wld"], wld0 + 7 * 16)
                g.collective_compute(
                    "AllGather", ALU.bypass,
                    replica_groups=[list(range(NCORES))],
                    ins=[T["cc_h_in"].opt()], outs=[T["cc_h_out"].opt()],
                ).then_inc(sems["cc"], 1)
                g.wait_ge(sems["cc"], cc0 + 1)
                g.dma_start(out=cc_sb[:, :],
                            in_=T["cc_h_out"]).then_inc(sems["wld"], 16)

            @blk.tensor
            def _(t):
                t.wait_ge(sems["wld"], wld0 + 8 * 16)
                for dc in range(8):
                    ins = t.transpose(phl[:, dc * 32:(dc + 1) * 32],
                                      cc_sb[:, dc * 128:(dc + 1) * 128],
                                      kb.idb[0:32, 0:32])
                ins.then_inc(sems["mm"], 1)

            @blk.vector
            def _(v):
                v.memset(halo_fm[:, :], 0.0)
                v.wait_ge(sems["ld"], ld0 + 3 * 16)
                v.wait_ge(sems["mm"], mmpre + 1)
                v.tensor_copy(halocand[:, :], phl[:, :])
                v.drain()
                for cp in range(NCORES):
                    src_ap = bass.AP(halocand, cp * 4,
                                     [[256, 128], [32, 8], [1, 4]])
                    ins = v.scalar_tensor_tensor(
                        halo_fm[:, :].rearrange("p (dc b) -> p dc b",
                                                dc=8, b=4),
                        src_ap, hflag[:, cp:cp + 1],
                        halo_fm[:, :].rearrange("p (dc b) -> p dc b",
                                                dc=8, b=4),
                        ALU.mult, ALU.add)
                    v.drain()
                ins.then_inc(sems["tp"], 1)
        cnt["ld"] += 3 * 16
        cnt["wld"] += 8 * 16
        cnt["mm"] += 1
        cnt["cc"] += 1
        cnt["tp"] += 1

        MMB, EVB, AUXB, GPB, STB = 28, 21, 8, 28, 64
        mm0, ev0, aux0, gp0, stq0 = (cnt["mm"], cnt["ev"], cnt["aux"], cnt["gp"],
                                     cnt["st"])
        with (nc.psum_tensor("pra", [128, 512], F32) as pra,
              nc.psum_tensor("prb", [128, 512], F32) as prb,
              nc.psum_tensor("pka", [128, 512], F32) as pka,
              nc.psum_tensor("pkb", [128, 512], F32) as pkb,
              nc.psum_tensor("pva", [128, 512], F32) as pva,
              nc.psum_tensor("pvb", [128, 512], F32) as pvb,
              nc.psum_tensor("pmt_d", [128, 1024], BF16) as pmt):
            with nc.Block("D1") as blk:
                @blk.tensor
                def _(t):
                    t.wait_ge(sems["wld"], wld0 + 3 * 16)
                    for b in range(B):
                        mmb, evb, auxb, gpb, stb = (mm0 + b * MMB, ev0 + b * EVB,
                                                    aux0 + b * AUXB, gp0 + b * GPB,
                                                    stq0 + b * STB)
                        t.wait_ge(sems["ev"], evb + 1)
                        for m in range(8):
                            if m >= 2:
                                t.wait_ge(sems["gp"], gpb + m - 1)
                            pr = pra if m % 2 == 0 else prb
                            for kc in range(8):
                                ins = t.matmul(pr[:, :],
                                               wr_sb[:, kc * 1024 + m * 128:
                                                     kc * 1024 + (m + 1) * 128],
                                               xr[:, kc * 512:(kc + 1) * 512],
                                               start=(kc == 0), stop=(kc == 7))
                            ins.then_inc(sems["mm"], 1)
                        for m in range(8):
                            if m >= 2:
                                t.wait_ge(sems["gp"], gpb + 8 + m - 1)
                            pk = pka if m % 2 == 0 else pkb
                            for kc in range(8):
                                ins = t.matmul(pk[:, :],
                                               wk_sb[:, kc * 1024 + m * 128:
                                                     kc * 1024 + (m + 1) * 128],
                                               xk[:, kc * 512:(kc + 1) * 512],
                                               start=(kc == 0), stop=(kc == 7))
                            ins.then_inc(sems["mm"], 1)
                        t.wait_ge(sems["ev"], evb + 9)
                        for m in range(8):
                            if m >= 2:
                                t.wait_ge(sems["ev"], evb + 9 + m - 1)
                            pv = pva if m % 2 == 0 else pvb
                            for kc in range(8):
                                ins = t.matmul(pv[:, :],
                                               wv_sb[:, kc * 1024 + m * 128:
                                                     kc * 1024 + (m + 1) * 128],
                                               kact[:, kc * 512:(kc + 1) * 512],
                                               start=(kc == 0), stop=(kc == 7))
                            ins.then_inc(sems["mm"], 1)
                        t.wait_ge(sems["ev"], evb + 17)
                        for tl in range(4):
                            if tl > 0:
                                t.wait_ge(sems["ev"], evb + 17 + tl)
                            for m in range(8):
                                ins = t.transpose(pmt[:, m * 128:(m + 1) * 128],
                                                  out_fm[:, m * 512 + tl * 128:
                                                         m * 512 + (tl + 1) * 128],
                                                  kb.idb[:, :])
                            ins.then_inc(sems["mm"], 1)

                @blk.vector
                def _(v):
                    for b in range(B):
                        mmb, evb, auxb, gpb, stb = (mm0 + b * MMB, ev0 + b * EVB,
                                                    aux0 + b * AUXB, gp0 + b * GPB,
                                                    stq0 + b * STB)
                        v.wait_ge(sems["tp"], tp0 + 1)
                        if b > 0:
                            v.wait_ge(sems["mm"], mmb - MMB + 24)
                        for dc in range(8):
                            v.tensor_copy(xp[:, dc * 512:dc * 512 + 1],
                                          halo_fm[:, dc * 4 + b:dc * 4 + b + 1])
                            v.tensor_copy(xp[:, dc * 512 + 1:(dc + 1) * 512],
                                          kb.fmbuf[:, dc * 2048 + b * 512:
                                                   dc * 2048 + b * 512 + 511])
                            v.tensor_tensor(dseg[:, :],
                                            kb.fmbuf[:, dc * 2048 + b * 512:
                                                     dc * 2048 + (b + 1) * 512],
                                            xp[:, dc * 512:(dc + 1) * 512],
                                            ALU.subtract)
                            v.scalar_tensor_tensor(xr[:, dc * 512:(dc + 1) * 512],
                                                   dseg[:, :],
                                                   mur_sb[:, dc:dc + 1],
                                                   xp[:, dc * 512:(dc + 1) * 512],
                                                   ALU.mult, ALU.add)
                            ins = v.scalar_tensor_tensor(
                                xk[:, dc * 512:(dc + 1) * 512], dseg[:, :],
                                muk_sb[:, dc:dc + 1],
                                xp[:, dc * 512:(dc + 1) * 512],
                                ALU.mult, ALU.add)
                        ins.then_inc(sems["ev"], 1)
                        for m in range(8):
                            v.wait_ge(sems["gp"], gpb + 8 + m + 1)
                            v.tensor_tensor(kact[:, m * 512:(m + 1) * 512],
                                            krelu[:, m * 512:(m + 1) * 512],
                                            krelu[:, m * 512:(m + 1) * 512],
                                            ALU.mult).then_inc(sems["ev"], 1)
                        for m in range(8):
                            v.wait_ge(sems["mm"], mmb + 16 + m + 1)
                            v.wait_ge(sems["gp"], gpb + m + 1)
                            pv = pva if m % 2 == 0 else pvb
                            v.tensor_tensor(out_fm[:, m * 512:(m + 1) * 512],
                                            pv[:, :], rbuf[:, m * 512:(m + 1) * 512],
                                            ALU.mult).then_inc(sems["ev"], 1)
                        for tl in range(4):
                            tile = b * 4 + tl
                            v.wait_ge(sems["mm"], mmb + 24 + tl + 1)
                            if tl > 0:
                                v.wait_ge(sems["gp"], gpb + 16 + 3 * tl)
                            elif b > 0:
                                v.wait_ge(sems["gp"], gpb)
                            v.tensor_tensor(kb.r32[:, 0:1024], pmt[:, :],
                                            kb.stream[:, tile * 1024:(tile + 1) * 1024],
                                            ALU.add)
                            ln_v1(v, kb, kb.r32[:, 0:1024]).then_inc(sems["aux"], 1)
                            v.wait_ge(sems["gp"], gpb + 16 + 3 * tl + 1)
                            ln_v2(v, kb).then_inc(sems["aux"], 1)
                            v.wait_ge(sems["gp"], gpb + 16 + 3 * tl + 2)
                            ln_v3(v, kb).then_inc(sems["ev"], 1)

                @blk.scalar
                def _(s):
                    for b in range(B):
                        mmb, evb, auxb, gpb, stb = (mm0 + b * MMB, ev0 + b * EVB,
                                                    aux0 + b * AUXB, gp0 + b * GPB,
                                                    stq0 + b * STB)
                        for m in range(8):
                            s.wait_ge(sems["mm"], mmb + m + 1)
                            pr = pra if m % 2 == 0 else prb
                            s.activation(rbuf[:, m * 512:(m + 1) * 512], pr[:, :],
                                         ACT.Sigmoid).then_inc(sems["gp"], 1)
                        for m in range(8):
                            s.wait_ge(sems["mm"], mmb + 8 + m + 1)
                            pk = pka if m % 2 == 0 else pkb
                            s.activation(krelu[:, m * 512:(m + 1) * 512], pk[:, :],
                                         ACT.Relu).then_inc(sems["gp"], 1)
                        for tl in range(4):
                            tile = b * 4 + tl
                            s.wait_ge(sems["aux"], auxb + 2 * tl + 1)
                            ln_s1(s, kb, kb.r32[:, 0:1024]).then_inc(sems["gp"], 1)
                            s.wait_ge(sems["aux"], auxb + 2 * tl + 2)
                            ln_s2_sqrt(s, kb).then_inc(sems["gp"], 1)
                            s.wait_ge(sems["ev"], evb + 17 + tl + 1)
                            gl_tl = b * 4 + tl
                            if gl_tl >= 2:
                                s.wait_ge(sems["st"], stq0 + (gl_tl - 1) * 16)
                            ln_apply(s, kb, kb.r32[:, 0:1024],
                                     yout[:, (tl % 2) * 1024:(tl % 2 + 1) * 1024]
                                     ).then_inc(sems["gp"], 1)

                @blk.sync
                def _(s):
                    for b in range(B):
                        gpb = gp0 + b * GPB
                        for tl in range(4):
                            tile = b * 4 + tl
                            s.wait_ge(sems["gp"], gpb + 16 + 3 * tl + 3)
                            s.dma_start(out=kb.y[tile * 128:(tile + 1) * 128, :],
                                        in_=yout[:, (tl % 2) * 1024:(tl % 2 + 1) * 1024]
                                        ).then_inc(sems["st"], 16)
                    s.wait_ge(sems["st"], stq0 + 16 * 16)

        cnt["mm"] += B * MMB
        cnt["ev"] += B * EVB
        cnt["aux"] += B * AUXB
        cnt["gp"] += B * GPB
        cnt["st"] += 16 * 16



def kernel(**inputs):
    import concourse.bass_utils as _bu
    in_maps = host_prepare(inputs)
    nc = build_program()
    res = _bu.run_bass_kernel_spmd(nc, in_maps, core_ids=list(range(NCORES)))
    out = np.empty((B, L, D), np.float32)
    for c in range(NCORES):
        out[:, c * POS:(c + 1) * POS, :] = res.results[c]["y"].reshape(B, POS, D)
    return out


# version 08 (final)
# speedup vs baseline: 1.0008x; 1.0008x over previous
import math
from contextlib import ExitStack

import numpy as np

import concourse.bass as bass
import concourse.mybir as mybir

F32 = mybir.dt.float32
BF16 = mybir.dt.bfloat16
ALU = mybir.AluOpType
ACT = mybir.ActivationFunctionType
AX = mybir.AxisListType

NCORES = 8
B, L, D, H = 4, 4096, 1024, 8
E = 128
POS = L // NCORES
TLOC = B * POS
NT = TLOC // 128
CH = 256
DH = 4096
EPS = 1e-5

GAMMAS = 1.0 - np.exp(np.linspace(math.log(1.0 / 32.0), math.log(1.0 / 512.0), H))



def _xpos_tables():
    pos = np.arange(L, dtype=np.float64)
    base = (np.arange(0, E, 2, dtype=np.float64) + 0.4 * E) / (1.4 * E)
    scale = base[None, :] ** (pos[:, None] / 512.0)
    inv_freq = 1.0 / 10000.0 ** (np.arange(64, dtype=np.float64) / 64.0)
    sinus = pos[:, None] * inv_freq[None, :]
    sin, cos = np.sin(sinus), np.cos(sinus)
    return cos * scale, sin * scale, cos / scale, sin / scale


def host_prepare(inputs):
    f32 = np.float32
    perm = np.concatenate([np.arange(0, E, 2), np.arange(1, E, 2)])
    perm_swap = np.concatenate([perm[64:], perm[:64]])

    WQ = np.asarray(inputs["L0_attn_W_Q"], f32)
    WK = np.asarray(inputs["L0_attn_W_K"], f32)
    WV = np.asarray(inputs["L0_attn_W_V"], f32)
    wq = np.ascontiguousarray(
        np.concatenate([WQ[:, :, perm], WQ[:, :, perm_swap]], axis=2))
    wkv = np.ascontiguousarray(np.concatenate([WK[:, :, perm], WV], axis=2))

    def dcmaj(w):
        N = w.shape[1]
        return np.ascontiguousarray(
            w.reshape(8, 128, N).transpose(1, 0, 2).reshape(128, 8 * N))

    wg = dcmaj(np.asarray(inputs["L0_attn_W_G"], f32))
    wo = dcmaj(np.asarray(inputs["L0_attn_W_O"], f32))
    gnw = np.ascontiguousarray(np.broadcast_to(
        np.asarray(inputs["L0_attn_gn_weight"], f32)[None, :], (128, D)))
    gnb = np.ascontiguousarray(np.broadcast_to(
        np.asarray(inputs["L0_attn_gn_bias"], f32)[None, :], (128, D)))

    def lhst_blocks(w, nm, nin):
        nic = nin // 128
        return np.ascontiguousarray(
            w.reshape(nm, 128, nic, 128).transpose(0, 3, 2, 1).reshape(nm, 128, nin))

    w1t = lhst_blocks(np.asarray(inputs["L0_ffn_w1_w"], f32), 32, 1024)
    w2t = lhst_blocks(np.asarray(inputs["L0_ffn_w2_w"], f32), 32, 1024)
    h1t = lhst_blocks(np.asarray(inputs["L0_ffn_h1_w"], f32), 32, 4096)
    h2t = lhst_blocks(np.asarray(inputs["L0_ffn_h2_w"], f32), 32, 4096)
    w3t = lhst_blocks(np.asarray(inputs["L0_ffn_w3_w"], f32), 8, 4096)

    def pscale(s):
        n = np.asarray(s).shape[0] // 128
        return np.ascontiguousarray(np.asarray(s, f32).reshape(n, 128).T)

    w1s = pscale(inputs["L0_ffn_w1_s"]); w2s = pscale(inputs["L0_ffn_w2_s"])
    h1s = pscale(inputs["L0_ffn_h1_s"]); h2s = pscale(inputs["L0_ffn_h2_s"])
    w3s = pscale(inputs["L0_ffn_w3_s"])

    kvq = np.stack([np.asarray(inputs["L1_attn_W_k"], f32),
                    np.asarray(inputs["L1_attn_W_v"], f32),
                    np.asarray(inputs["L1_attn_W_q"], f32)], 0)
    wkvq2 = np.ascontiguousarray(
        kvq.transpose(2, 0, 1)
        .reshape(8, 128, 3, 1024)
        .transpose(1, 0, 2, 3)
        .reshape(128, 8 * 3 * 1024))

    def dlhst(w):
        return np.ascontiguousarray(
            w.T.reshape(8, 128, 1024).transpose(1, 0, 2).reshape(128, 8192))

    wr = dlhst(np.asarray(inputs["L1_ffn_W_r"], f32))
    wkd = dlhst(np.asarray(inputs["L1_ffn_W_k"], f32))
    wvd = dlhst(np.asarray(inputs["L1_ffn_W_v"], f32))
    mur = pscale(inputs["L1_ffn_mu_r"])
    muk = pscale(inputs["L1_ffn_mu_k"])

    ident = np.eye(128, dtype=f32)
    r = np.arange(CH); s = np.arange(128)
    m0 = (s[:, None] <= r[None, :]).astype(f32)
    m1 = ((s[:, None] + 128) <= r[None, :]).astype(f32)

    cq, sq, ck, sk = _xpos_tables()
    X = np.asarray(inputs["L0_attn_X"], f32)

    shared = dict(wq=wq, wkv=wkv, wg=wg, wo=wo, gnw=gnw, gnb=gnb,
                  w1t=w1t, w2t=w2t, h1t=h1t, h2t=h2t, w3t=w3t,
                  w1s=w1s, w2s=w2s, h1s=h1s, h2s=h2s, w3s=w3s,
                  wkvq2=wkvq2, wr=wr, wkd=wkd, wvd=wvd, mur=mur, muk=muk,
                  ident=ident, m0=m0, m1=m1)

    i_of_p = np.arange(128) % 64
    sgn = np.where(np.arange(128) < 64, -1.0, 1.0)

    in_maps = []
    for c in range(NCORES):
        p0 = c * POS
        posg = np.arange(p0, p0 + POS)
        x0 = np.ascontiguousarray(X[:, p0:p0 + POS, :].reshape(TLOC, D))

        cqfm = np.ascontiguousarray(cq[posg][:, i_of_p].T).astype(f32)
        sqfm = np.ascontiguousarray((sq[posg][:, i_of_p] * sgn[None, :]).T).astype(f32)

        ckh = np.zeros((128, H * 512), np.float64)
        skh = np.zeros((128, H * 512), np.float64)
        for h in range(H):
            g = GAMMAS[h]
            for pt in range(4):
                pg = posg[pt * 128:(pt + 1) * 128]
                dec = g ** (CH - (pg % CH))
                c0 = h * 512 + pt * 128
                ckh[:, c0:c0 + 128] = ck[pg][:, i_of_p] * dec[:, None]
                skh[:, c0:c0 + 128] = sk[pg][:, i_of_p] * sgn[None, :] * dec[:, None]
        ckh = ckh.astype(f32); skh = skh.astype(f32)

        ysc = np.zeros((128, 32), np.float64)
        for h in range(H):
            for pt in range(4):
                pg = posg[pt * 128:(pt + 1) * 128]
                ysc[:, h * 4 + pt] = GAMMAS[h] ** ((pg % CH).astype(np.float64) - CH)
        ysc = ysc.astype(f32)

        wpref = np.zeros((128, 64), np.float64)
        for cp in range(c):
            for h in range(H):
                wpref[:, cp * 8 + h] = GAMMAS[h] ** (2.0 * CH * (c - 1 - cp))
        wpref = wpref.astype(f32)

        haloflag = np.zeros((128, 8), f32)
        if c > 0:
            haloflag[:, c - 1] = 1.0

        m = dict(shared)
        m.update(x0=x0, cqfm=cqfm, sqfm=sqfm, ckh=ckh, skh=skh, ysc=ysc,
                 wpref=wpref, haloflag=haloflag)
        in_maps.append(m)
    return in_maps



def swap_ap(ap128):
    t = ap128.tensor
    pstep = ap128.ap[0][0]
    return bass.AP(t, ap128.offset + 64, [[pstep, 128], [-64, 2], [1, 64]])


class KB:

    def __init__(self, nc, es):
        self.nc = nc
        self.es = es
        self.sems = {}
        self.cnt = {}
        for nm in ["ld", "wld", "wlh", "mm", "ev", "sc", "st", "cc", "tp", "aux",
                   "q", "sm", "ym", "tst", "gp", "cst"]:
            self.sems[nm] = es.enter_context(nc.semaphore(nm))
            self.cnt[nm] = 0


def build_program(stop_phase="D"):
    nc = bass.Bass("TRN2", target_bir_lowering=False)

    def din(name, shape, dtype=F32):
        return nc.dram_tensor(name, list(shape), dtype, kind="ExternalInput").ap()

    T = {}
    T["x0"] = din("x0", [TLOC, D])
    T["wq"] = din("wq", [H, D, 256]); T["wkv"] = din("wkv", [H, D, 256])
    T["wg"] = din("wg", [128, 8 * D]); T["wo"] = din("wo", [128, 8 * D])
    T["gnw"] = din("gnw", [128, D]); T["gnb"] = din("gnb", [128, D])
    T["w1t"] = din("w1t", [32, 128, 1024]); T["w2t"] = din("w2t", [32, 128, 1024])
    T["h1t"] = din("h1t", [32, 128, 4096]); T["h2t"] = din("h2t", [32, 128, 4096])
    T["w3t"] = din("w3t", [8, 128, 4096])
    T["w1s"] = din("w1s", [128, 32]); T["w2s"] = din("w2s", [128, 32])
    T["h1s"] = din("h1s", [128, 32]); T["h2s"] = din("h2s", [128, 32])
    T["w3s"] = din("w3s", [128, 8])
    T["wkvq2"] = din("wkvq2", [128, 24576])
    T["wr"] = din("wr", [128, 8192]); T["wkd"] = din("wkd", [128, 8192])
    T["wvd"] = din("wvd", [128, 8192])
    T["mur"] = din("mur", [128, 8]); T["muk"] = din("muk", [128, 8])
    T["ident"] = din("ident", [128, 128])
    T["m0"] = din("m0", [128, 256]); T["m1"] = din("m1", [128, 256])
    T["cqfm"] = din("cqfm", [128, 512]); T["sqfm"] = din("sqfm", [128, 512])
    T["ckh"] = din("ckh", [128, H * 512]); T["skh"] = din("skh", [128, H * 512])
    T["ysc"] = din("ysc", [128, 32]); T["wpref"] = din("wpref", [128, 64])
    T["haloflag"] = din("haloflag", [128, 8])

    y = nc.dram_tensor("y", [TLOC, D], F32, kind="ExternalOutput").ap()

    T["cc_a_in"] = nc.dram_tensor("cc_a_in", [128, H * 512], F32).ap()
    T["cc_a_out"] = nc.dram_tensor("cc_a_out", [NCORES * 128, H * 512], F32).ap()
    T["cc_h_in"] = nc.dram_tensor("cc_h_in", [B, D], F32).ap()
    T["cc_h_out"] = nc.dram_tensor("cc_h_out", [NCORES * B, D], F32).ap()
    T["wkvq2_bf"] = nc.dram_tensor("wkvq2_bf", [128, 24576], BF16).ap()
    T["w1t_bf"] = nc.dram_tensor("w1t_bf", [32, 128, 1024], BF16).ap()
    T["w2t_bf"] = nc.dram_tensor("w2t_bf", [32, 128, 1024], BF16).ap()
    T["h1t_bf"] = nc.dram_tensor("h1t_bf", [32, 128, 4096], BF16).ap()
    T["h2t_bf"] = nc.dram_tensor("h2t_bf", [32, 128, 4096], BF16).ap()
    T["w3t_bf"] = nc.dram_tensor("w3t_bf", [8, 128, 4096], BF16).ap()
    T["wg_bf"] = nc.dram_tensor("wg_bf", [128, 8192], BF16).ap()
    T["wo_bf"] = nc.dram_tensor("wo_bf", [128, 8192], BF16).ap()

    with ExitStack() as es:
        kb = KB(nc, es)
        kb.T = T
        kb.y = y

        kb.stream = es.enter_context(nc.sbuf_tensor("stream", [128, NT * 1024], BF16))
        kb.fmbuf = es.enter_context(nc.sbuf_tensor("fmbuf", [128, NT * 1024], BF16))
        kb.idb = es.enter_context(nc.sbuf_tensor("idb", [128, 128], BF16))
        kb.lnv = es.enter_context(nc.sbuf_tensor("lnv", [128, 32], F32))
        kb.trash = es.enter_context(nc.sbuf_tensor("trash", [128, 2048], BF16))
        kb.r32 = es.enter_context(nc.sbuf_tensor("r32", [128, 2048], F32))

        emit_phase_a(kb)
        if stop_phase == "A":
            _debug_out(kb)
            return nc
        emit_phase_b(kb)
        if stop_phase == "B":
            _debug_out(kb)
            return nc
        emit_phase_c(kb)
        if stop_phase == "C":
            _debug_out(kb)
            return nc
        emit_phase_d(kb)
    return nc


def _debug_out(kb):
    nc, sems, cnt = kb.nc, kb.sems, kb.cnt
    with nc.Block("dbgout") as blk:
        @blk.gpsimd
        def _(g):
            for t in range(NT):
                g.dma_start(out=kb.y[t * 128:(t + 1) * 128, :],
                            in_=kb.stream[:, t * 1024:(t + 1) * 1024]).then_inc(sems["st"], 16)
            g.wait_ge(sems["st"], cnt["st"] + 16 * NT)
    cnt["st"] += 16 * NT



def emit_tm_to_fm(kb, name):
    nc, sems, cnt = kb.nc, kb.sems, kb.cnt
    with nc.psum_tensor("ptr_" + name, [128, 1024], BF16) as ptr, \
         nc.psum_tensor("ptr2_" + name, [128, 1024], BF16) as ptr2:
        with nc.Block(name) as blk:
            @blk.tensor
            def _(t):
                for grp in range(16):
                    dc, tg = grp // 2, grp % 2
                    pp = ptr if grp % 2 == 0 else ptr2
                    if grp >= 2:
                        t.wait_ge(sems["tp"], cnt["tp"] + grp - 1)
                    for tt in range(8):
                        t0 = tg * 8 + tt
                        ins = t.transpose(pp[:, tt * 128:(tt + 1) * 128],
                                          kb.stream[:, t0 * 1024 + dc * 128:
                                                    t0 * 1024 + dc * 128 + 128],
                                          kb.idb[:, :])
                    ins.then_inc(sems["mm"], 1)

            @blk.vector
            def _(v):
                for grp in range(16):
                    dc, tg = grp // 2, grp % 2
                    pp = ptr if grp % 2 == 0 else ptr2
                    v.wait_ge(sems["mm"], cnt["mm"] + grp + 1)
                    v.tensor_copy(kb.fmbuf[:, dc * 2048 + tg * 1024:
                                           dc * 2048 + (tg + 1) * 1024],
                                  pp[:, :]).then_inc(sems["tp"], 1)
    cnt["mm"] += 16
    cnt["tp"] += 16



def ln_v1(v, kb, resid, sl=0):
    o = sl * 16
    return v.tensor_scalar(kb.trash[:, sl * 1024:(sl + 1) * 1024], resid, 1.0,
                           None, ALU.mult, ALU.add,
                           accum_out=kb.lnv[:, o + 0:o + 1])


def ln_s1(s, kb, resid, sl=0):
    o = sl * 16
    return s.activation(kb.trash[:, sl * 1024:(sl + 1) * 1024], resid, ACT.Square,
                        accum_out=kb.lnv[:, o + 1:o + 2])


def ln_v2(v, kb, sl=0):
    o = sl * 16
    lnv = kb.lnv
    v.drain()
    v.tensor_scalar(lnv[:, o + 2:o + 3], lnv[:, o + 0:o + 1], 1.0 / 1024.0,
                    None, ALU.mult)
    v.tensor_scalar(lnv[:, o + 3:o + 4], lnv[:, o + 1:o + 2], 1.0 / 1024.0,
                    None, ALU.mult)
    v.drain()
    v.tensor_tensor(lnv[:, o + 7:o + 8], lnv[:, o + 2:o + 3],
                    lnv[:, o + 2:o + 3], ALU.mult)
    v.drain()
    v.tensor_tensor(lnv[:, o + 3:o + 4], lnv[:, o + 3:o + 4],
                    lnv[:, o + 7:o + 8], ALU.subtract)
    v.drain()
    v.tensor_scalar(lnv[:, o + 3:o + 4], lnv[:, o + 3:o + 4], 0.0, EPS,
                    ALU.max, ALU.add)
    v.drain()
    return v.reciprocal(lnv[:, o + 4:o + 5], lnv[:, o + 3:o + 4])


def ln_s2_sqrt(s, kb, sl=0):
    o = sl * 16
    s.drain()
    return s.activation(kb.lnv[:, o + 5:o + 6], kb.lnv[:, o + 4:o + 5], ACT.Sqrt)


def ln_v3(v, kb, sl=0):
    o = sl * 16
    v.drain()
    return v.tensor_scalar(kb.lnv[:, o + 6:o + 7], kb.lnv[:, o + 2:o + 3],
                           kb.lnv[:, o + 5:o + 6], -1.0, ALU.mult, ALU.mult)


def ln_apply(s, kb, resid, out, sl=0):
    o = sl * 16
    s.drain()
    return s.activation(out, resid, ACT.Identity, bias=kb.lnv[:, o + 6:o + 7],
                        scale=kb.lnv[:, o + 5:o + 6])



def emit_kv_head_T(t, kb, h, wkv_sb, pkv, pkv2, ktm, vtm_h, ev_base, mm_sem, ev_sem):
    sems = kb.sems
    for tt in range(NT):
        ps = pkv if tt % 2 == 0 else pkv2
        t.wait_ge(ev_sem, ev_base + max(0, tt - 1))
        for dc in range(8):
            ins = t.matmul(ps[:, :],
                           kb.fmbuf[:, dc * 2048 + tt * 128:dc * 2048 + tt * 128 + 128],
                           wkv_sb[:, dc * 256:(dc + 1) * 256],
                           start=(dc == 0), stop=(dc == 7))
        ins.then_inc(mm_sem, 1)


def emit_kv_head_V(v, kb, h, pkv, pkv2, ktm, vtm_t, vcol0, ckh_sb, skh_sb,
                   ktmp1, ktmp2, mm_base, ev_sem, mm_sem):
    for tt in range(NT):
        ps = pkv if tt % 2 == 0 else pkv2
        pt = tt % 4
        v.wait_ge(mm_sem, mm_base + tt + 1)
        ck_ap = ckh_sb[:, h * 512 + pt * 128:h * 512 + pt * 128 + 128]
        sk_ap = skh_sb[:, h * 512 + pt * 128:h * 512 + pt * 128 + 128]
        sl_ = (tt % 2) * 128
        k1 = ktmp1[:, sl_:sl_ + 128]
        k2 = ktmp2[:, sl_:sl_ + 128]
        v.tensor_tensor(k1, ps[:, 0:128], ck_ap, ALU.mult)
        v.tensor_tensor(k2, swap_ap(ps[:, 0:128]), sk_ap, ALU.mult)
        v.drain()
        v.tensor_tensor(ktm[:, tt * 128:(tt + 1) * 128], k1, k2,
                        ALU.add)
        v.tensor_copy(vtm_t[:, vcol0 + tt * 128:vcol0 + (tt + 1) * 128],
                      ps[:, 128:256]).then_inc(ev_sem, 1)


SUBSTOP = "FULL"
DBG = False


def emit_phase_a(kb):
    nc, sems, cnt, T = kb.nc, kb.sems, kb.cnt, kb.T

    with ExitStack() as aes:
        sb = lambda name, shp, dt=BF16: aes.enter_context(nc.sbuf_tensor(name, shp, dt))
        cqfm_sb = sb("cqfm_sb", [128, 512])
        sqfm_sb = sb("sqfm_sb", [128, 512])
        ckh_sb = sb("ckh_sb", [128, H * 512])
        skh_sb = sb("skh_sb", [128, H * 512])
        m0_sb = sb("m0_sb", [128, 256]); m1_sb = sb("m1_sb", [128, 256])
        gnw_sb = sb("gnw_sb", [128, D])
        gnb_sb = sb("gnb_sb", [128, D])
        ysc_sb = sb("ysc_sb", [128, 32], F32)
        wpref_sb = sb("wpref_sb", [128, 64], F32)
        yn = sb("yn", [128, NT * 1024])
        tinit_all = sb("tinit_all", [128, H * B * 128])
        tstate = sb("tstate", [128, B * 128], F32)
        tstate_bf = sb("tstate_bf", [128, B * 128])
        asb = sb("asb", [128, B * 128], F32)
        sm_sb = sb("sm_sb", [128, 2 * 512])

        ld0, wld0 = cnt["ld"], cnt["wld"]
        with nc.Block("A0a") as blk:
            @blk.sync
            def _(s):
                for ap, src in [(ysc_sb, "ysc"), (wpref_sb, "wpref")]:
                    s.dma_start(out=ap[:, :], in_=T[src]).then_inc(sems["ld"], 16)
                s.wait_ge(sems["ld"], ld0 + 2 * 16)

            @blk.gpsimd
            def _(g):
                g.dma_start(out=kb.stream[:, :].rearrange("p (t d) -> p t d", t=NT),
                            in_=T["x0"].rearrange("(t p) d -> p t d", p=128)
                            ).then_inc(sems["wld"], 16)
                for ap, src in [(kb.idb, "ident"), (ckh_sb, "ckh"), (skh_sb, "skh"),
                                (m0_sb, "m0"), (m1_sb, "m1"),
                                (gnw_sb, "gnw"), (gnb_sb, "gnb"),
                                (cqfm_sb, "cqfm"), (sqfm_sb, "sqfm")]:
                    g.dma_start(out=ap[:, :], in_=T[src]).then_inc(sems["wld"], 16)
                g.wait_ge(sems["wld"], wld0 + 10 * 16)
        cnt["ld"] += 2 * 16
        cnt["wld"] += 10 * 16

        emit_tm_to_fm(kb, "A0b")
        if SUBSTOP == "A0":
            return

        mm0, ev0, aux0, wld0, st0 = (cnt["mm"], cnt["ev"], cnt["aux"],
                                     cnt["wld"], cnt["st"])
        with ExitStack() as a1s:
            sb1 = lambda name, shp, dt=BF16: a1s.enter_context(
                nc.sbuf_tensor(name, shp, dt))
            ktm = sb1("ktm_a1", [128, 2048])
            vtm1 = sb1("vtm_a1", [128, 2048])
            ktmp1 = sb1("ktmp1_a1", [128, 256], F32)
            ktmp2 = sb1("ktmp2_a1", [128, 256], F32)
            wkv_sb = sb1("wkv_a1", [128, 2048])
            i2a = sb1("i2a", [128, 256], F32)
            with (nc.psum_tensor("pkv", [128, 256], F32) as pkv,
                  nc.psum_tensor("pkv2", [128, 256], F32) as pkv2,
                  nc.psum_tensor("pinc", [128, 128], F32) as pinc,
                  nc.psum_tensor("pinc2", [128, 128], F32) as pinc2):
                with nc.Block("A1") as blk:
                    @blk.gpsimd
                    def _(g):
                        for h in range(H):
                            if h >= 1:
                                g.wait_ge(sems["mm"], mm0 + h * 16)
                            g.dma_start(out=wkv_sb[:, :].rearrange(
                                            "p (dc n) -> p dc n", dc=8),
                                        in_=T["wkv"][h].rearrange(
                                            "(dc p) n -> p dc n", p=128)
                                        ).then_inc(sems["wld"], 16)

                    @blk.tensor
                    def _(t):
                        for h in range(H):
                            t.wait_ge(sems["wld"], wld0 + (h + 1) * 16)
                            emit_kv_head_T(t, kb, h, wkv_sb, pkv, pkv2, ktm, vtm1,
                                           ev0 + h * 25, sems["mm"], sems["ev"])
                            for b in range(B):
                                for ci in range(2):
                                    cx = b * 2 + ci
                                    pp = pinc if cx % 2 == 0 else pinc2
                                    t.wait_ge(sems["ev"],
                                              ev0 + h * 25 + 16 + max(0, cx - 1))
                                    for sub in range(2):
                                        tt = b * 4 + ci * 2 + sub
                                        ins = t.matmul(pp[:, :],
                                                       ktm[:, tt * 128:(tt + 1) * 128],
                                                       vtm1[:, tt * 128:(tt + 1) * 128],
                                                       start=(sub == 0), stop=(sub == 1))
                                    ins.then_inc(sems["aux"], 1)

                    @blk.vector
                    def _(v):
                        for h in range(H):
                            emit_kv_head_V(v, kb, h, pkv, pkv2, ktm, vtm1, 0,
                                           ckh_sb, skh_sb, ktmp1, ktmp2,
                                           mm0 + h * 16, sems["ev"], sems["mm"])
                            gC = float(GAMMAS[h] ** CH)
                            if h > 0:
                                v.wait_ge(sems["st"], st0 + h * 16)
                            for b in range(B):
                                for ci in range(2):
                                    cx = b * 2 + ci
                                    pp = pinc if cx % 2 == 0 else pinc2
                                    v.wait_ge(sems["aux"], aux0 + h * 8 + cx + 1)
                                    v.tensor_scalar(i2a[:, ci * 128:(ci + 1) * 128],
                                                    pp[:, :], gC, None,
                                                    ALU.mult).then_inc(sems["ev"], 1)
                                v.drain()
                                v.scalar_tensor_tensor(
                                    asb[:, b * 128:(b + 1) * 128],
                                    i2a[:, 0:128], gC, i2a[:, 128:256],
                                    ALU.mult, ALU.add)
                                v.drain()
                            v.sem_inc(sems["ev"], 1)

                    @blk.sync
                    def _(s):
                        for h in range(H):
                            s.wait_ge(sems["ev"], ev0 + (h + 1) * 25)
                            s.dma_start(out=T["cc_a_in"][:, h * 512:(h + 1) * 512],
                                        in_=asb[:, :]).then_inc(sems["st"], 16)
                            s.wait_ge(sems["st"], st0 + (h + 1) * 16)
        cnt["mm"] += H * 16
        cnt["aux"] += H * 8
        cnt["ev"] += H * 25
        cnt["wld"] += H * 16
        cnt["st"] += H * 16
        if SUBSTOP == "A1":
            return

        cst0 = cnt["cst"]
        mm0, ev0, aux0, q0, sm0, ym0, tst0, st0, cc0, wld0, sc0, tp0, gp0 = (
            cnt["mm"], cnt["ev"], cnt["aux"], cnt["q"], cnt["sm"], cnt["ym"],
            cnt["tst"], cnt["st"], cnt["cc"], cnt["wld"], cnt["sc"], cnt["tp"],
            cnt["gp"])
        DO_HEADS = SUBSTOP not in ("A2coll",)
        DO_TR = SUBSTOP not in ("A2coll", "A2h0")
        DO_INC = DO_TR and SUBSTOP != "A2h0t"
        DO_QP = DO_INC and SUBSTOP != "A2h0i"
        DO_CHUNKS = DO_QP and SUBSTOP not in ("A2h1",)
        NH = H if DO_HEADS else 0
        MMH = 16 + (2 if DO_TR else 0) + (4 if DO_QP else 0)
        EVH = 16 + (2 if DO_TR else 0) + (8 if DO_INC else 0) + \
            (2 if DO_CHUNKS else 0)
        with ExitStack() as a2s:
            sb2 = lambda name, shp, dt=BF16: a2s.enter_context(
                nc.sbuf_tensor(name, shp, dt))
            ktm = sb2("ktm_a2", [128, 2048])
            ktmp1 = sb2("ktmp1_a2", [128, 256], F32)
            ktmp2 = sb2("ktmp2_a2", [128, 256], F32)
            wkv_sb = sb2("wkv_a2", [128, 2048])
            wq_sb = sb2("wq_a2", [128, 2048])
            qfm = sb2("qfm", [128, 2 * 2048])
            qtmp1 = sb2("qtmp1", [128, 512], F32)
            qtmp2 = sb2("qtmp2", [128, 512], F32)
            kfm2 = sb2("kfm2", [128, 2 * 2048])
            vtm2 = sb2("vtm2", [128, 2 * 2048])
            i2 = sb2("i2", [128, 8 * 128])
            pfx = sb2("pfx", [128, 2 * 512], F32)
            ystage = sb2("ystage", [128, 2048], F32)
            ysq = kb.r32
            gstat = sb2("gstat", [128, 112], F32)
            with (nc.psum_tensor("pkv_a2", [128, 256], F32) as pkv,
                  nc.psum_tensor("pkv2_a2", [128, 256], F32) as pkv2,
                  nc.psum_tensor("ptr1", [128, 1024], BF16) as ptr1,
                  nc.psum_tensor("pq1", [128, 512], F32) as pq1,
                  nc.psum_tensor("pq2", [128, 512], F32) as pq2,
                  nc.psum_tensor("psc", [128, 512], F32) as psc,
                  nc.psum_tensor("psc2", [128, 512], F32) as psc2,
                  nc.psum_tensor("py12", [128, 256], F32) as py12):
                pinc, pinc2 = pq1, pq2
                with nc.Block("A2") as blk:
                    @blk.gpsimd
                    def _(g):
                        g.wait_ge(sems["st"], st0)
                        g.collective_compute(
                            "AllGather", ALU.bypass,
                            replica_groups=[list(range(NCORES))],
                            ins=[T["cc_a_in"].opt()], outs=[T["cc_a_out"].opt()],
                        ).then_inc(sems["cc"], 1)
                        for h in range(NH):
                            if h >= 1:
                                g.wait_ge(sems["mm"], mm0 + (h - 1) * MMH + 16)
                            if h >= 2 and DO_CHUNKS:
                                g.wait_ge(sems["ym"], ym0 + (h - 2) * 8 + 8)
                            g.dma_start(out=wkv_sb[:, :].rearrange(
                                            "p (dc n) -> p dc n", dc=8),
                                        in_=T["wkv"][h].rearrange(
                                            "(dc p) n -> p dc n", p=128)
                                        ).then_inc(sems["wld"], 16)
                            if h >= 1:
                                g.wait_ge(sems["mm"], mm0 + (h - 1) * MMH + 22)
                            g.dma_start(out=wq_sb[:, :].rearrange(
                                            "p (dc n) -> p dc n", dc=8),
                                        in_=T["wq"][h].rearrange(
                                            "(dc p) n -> p dc n", p=128)
                                        ).then_inc(sems["wld"], 16)
                        g.dma_start(out=T["wg_bf"].rearrange("p n -> (p n)"),
                                    in_=T["wg"].rearrange("p n -> (p n)")
                                    ).then_inc(sems["cst"], 16)
                        g.dma_start(out=T["wo_bf"].rearrange("p n -> (p n)"),
                                    in_=T["wo"].rearrange("p n -> (p n)")
                                    ).then_inc(sems["cst"], 16)
                        g.wait_ge(sems["cst"], cst0 + 32)
                        for srcn, dstn, nm in [("w1t", "w1t_bf", 32),
                                               ("w2t", "w2t_bf", 32),
                                               ("w3t", "w3t_bf", 8),
                                               ("h1t", "h1t_bf", 32)]:
                            for m in range(nm):
                                g.dma_start(out=T[dstn][m].rearrange("p n -> (p n)"),
                                            in_=T[srcn][m].rearrange("p n -> (p n)")
                                            ).then_inc(sems["cst"], 16)

                    @blk.sync
                    def _(s):
                        s.wait_ge(sems["cc"], cc0 + 1)
                        for h in range(H):
                            for cp in range(NCORES):
                                li = h * NCORES + cp
                                pslot = pfx[:, (li % 2) * 512:(li % 2 + 1) * 512]
                                s.wait_ge(sems["tst"], tst0 + max(0, li - 1))
                                s.dma_start(
                                    out=pslot,
                                    in_=T["cc_a_out"][cp * 128:(cp + 1) * 128,
                                                      h * 512:(h + 1) * 512]
                                    ).then_inc(sems["st"], 16)

                    @blk.tensor
                    def _(t):
                        for h in range(NH):
                            t.wait_ge(sems["wld"], wld0 + h * 32 + 32)
                            emit_kv_head_T(t, kb, h, wkv_sb, pkv, pkv2, ktm,
                                           vtm2, ev0 + h * EVH, sems["mm"],
                                           sems["ev"])
                            for grp in range(2 if DO_TR else 0):
                                t.wait_ge(sems["ev"], ev0 + h * EVH + (grp + 1) * 8)
                                if grp == 1:
                                    t.wait_ge(sems["ev"], ev0 + h * EVH + 17)
                                for tt in range(8):
                                    t0 = grp * 8 + tt
                                    ins = t.transpose(
                                        pp_slice(ptr1, tt),
                                        ktm[:, t0 * 128:(t0 + 1) * 128],
                                        kb.idb[:, :])
                                ins.then_inc(sems["mm"], 1)
                            for b in range(B if DO_INC else 0):
                                for ci in range(2):
                                    cx = b * 2 + ci
                                    pp = pinc if cx % 2 == 0 else pinc2
                                    t.wait_ge(sems["ev"],
                                              ev0 + h * EVH + 18 + max(0, cx - 1))
                                    for sub in range(2):
                                        tt = b * 4 + ci * 2 + sub
                                        ins = t.matmul(
                                            pp[:, 0:128],
                                            ktm[:, tt * 128:(tt + 1) * 128],
                                            vtm2[:, (h % 2) * 2048 + tt * 128:
                                                 (h % 2) * 2048 + (tt + 1) * 128],
                                            start=(sub == 0), stop=(sub == 1))
                                    ins.then_inc(sems["aux"], 1)
                            if not DO_QP:
                                continue
                            t.wait_ge(sems["wld"], wld0 + h * 32 + 32)
                            t.wait_ge(sems["ev"], ev0 + h * EVH + 26)
                            if h >= 2 and DO_CHUNKS:
                                t.wait_ge(sems["ym"], ym0 + (h - 2) * 8 + 8)
                            for n in range(4):
                                t.wait_ge(sems["q"], q0 + h * 4 + n)
                                for dc in range(8):
                                    t.matmul(pq1[:, :],
                                             wq_sb[:, dc * 256:dc * 256 + 128],
                                             kb.fmbuf[:, dc * 2048 + n * 512:
                                                      dc * 2048 + (n + 1) * 512],
                                             start=(dc == 0), stop=(dc == 7))
                                for dc in range(8):
                                    ins = t.matmul(pq2[:, :],
                                                   wq_sb[:, dc * 256 + 128:
                                                         dc * 256 + 256],
                                                   kb.fmbuf[:, dc * 2048 + n * 512:
                                                            dc * 2048 + (n + 1) * 512],
                                                   start=(dc == 0), stop=(dc == 7))
                                ins.then_inc(sems["mm"], 1)
                            if not DO_CHUNKS:
                                continue
                            t.wait_ge(sems["ev"], ev0 + h * EVH + 18)
                            for ci in range(2):
                                for b in range(B):
                                    cx = ci * B + b
                                    gx = h * 8 + cx
                                    qc0 = (h % 2) * 2048 + b * 512 + ci * 256
                                    t.wait_ge(sems["q"], q0 + h * 4 + b + 1)
                                    t.wait_ge(sems["sc"], sc0 + max(0, gx - 1))
                                    pscx = psc if gx % 2 == 0 else psc2
                                    for ks in range(2):
                                        tt = b * 4 + ci * 2 + ks
                                        ins = t.matmul(
                                            pscx[:, ks * 256:(ks + 1) * 256],
                                            kfm2[:, (h % 2) * 2048 + tt * 128:
                                                 (h % 2) * 2048 + (tt + 1) * 128],
                                            qfm[:, qc0:qc0 + 256],
                                            start=True, stop=True)
                                    ins.then_inc(sems["sm"], 1)
                                    t.wait_ge(sems["sc"], sc0 + gx + 1)
                                    if ci == 0:
                                        t.wait_ge(sems["tst"],
                                                  tst0 + 64 + h * 9 + 1)
                                    else:
                                        t.wait_ge(sems["tst"],
                                                  tst0 + 64 + h * 9 + 2 + b)
                                    if gx > 0:
                                        t.wait_ge(sems["tp"], tp0 + gx)
                                    for qs in range(2):
                                        pyap = py12[:, qs * 128:(qs + 1) * 128]
                                        t.matmul(pyap,
                                                 qfm[:, qc0 + qs * 128:
                                                     qc0 + qs * 128 + 128],
                                                 tstate_bf[:, b * 128:(b + 1) * 128],
                                                 start=True, stop=False)
                                        for ks in range(qs + 1):
                                            tt = b * 4 + ci * 2 + ks
                                            ins = t.matmul(
                                                pyap,
                                                sm_sb[:, (gx % 2) * 512 + ks * 256 +
                                                      qs * 128:(gx % 2) * 512 +
                                                      ks * 256 + qs * 128 + 128],
                                                vtm2[:, (h % 2) * 2048 + tt * 128:
                                                     (h % 2) * 2048 + (tt + 1) * 128],
                                                start=False, stop=(ks == qs))
                                    ins.then_inc(sems["ym"], 1)

                    @blk.vector
                    def _(v):
                        if not DO_HEADS:
                            for hh in range(H):
                                for cp in range(NCORES):
                                    li = hh * NCORES + cp
                                    pslot = pfx[:, (li % 2) * 512:(li % 2 + 1) * 512]
                                    v.wait_ge(sems["st"],
                                              st0 + ((li // 2) + 1) * 32)
                                    dst = tinit_all[:, hh * 512:(hh + 1) * 512]
                                    wv = wpref_sb[:, cp * 8 + hh:cp * 8 + hh + 1]
                                    if cp == 0:
                                        v.tensor_scalar(dst, pslot, wv, None,
                                                        ALU.mult).then_inc(sems["tst"], 1)
                                    else:
                                        v.drain()
                                        v.scalar_tensor_tensor(dst, pslot, wv, dst,
                                                               ALU.mult,
                                                               ALU.add).then_inc(sems["tst"], 1)
                        for h in range(NH):
                            emit_kv_head_V(v, kb, h, pkv, pkv2, ktm,
                                           vtm2, (h % 2) * 2048,
                                           ckh_sb, skh_sb, ktmp1, ktmp2,
                                           mm0 + h * MMH, sems["ev"], sems["mm"])
                            for grp in range(2 if DO_TR else 0):
                                v.wait_ge(sems["mm"], mm0 + h * MMH + 16 + grp + 1)
                                v.tensor_copy(
                                    kfm2[:, (h % 2) * 2048 + grp * 1024:
                                         (h % 2) * 2048 + (grp + 1) * 1024],
                                    ptr1[:, :]).then_inc(sems["ev"], 1)
                            gC = float(GAMMAS[h] ** CH)
                            for b in range(B if DO_INC else 0):
                                for ci in range(2):
                                    cx = b * 2 + ci
                                    pp = pinc if cx % 2 == 0 else pinc2
                                    v.wait_ge(sems["aux"], aux0 + h * 8 + cx + 1)
                                    v.tensor_scalar(i2[:, cx * 128:(cx + 1) * 128],
                                                    pp[:, 0:128], gC, None,
                                                    ALU.mult).then_inc(sems["ev"], 1)
                            for n in range(4 if DO_QP else 0):
                                v.wait_ge(sems["mm"], mm0 + h * MMH + 18 + n + 1)
                                v.drain()
                                v.tensor_tensor(qtmp1[:, :], pq1[:, :],
                                                cqfm_sb[:, :], ALU.mult)
                                v.tensor_tensor(qtmp2[:, :], pq2[:, :],
                                                sqfm_sb[:, :], ALU.mult)
                                v.drain()
                                v.tensor_tensor(qfm[:, (h % 2) * 2048 + n * 512:
                                                    (h % 2) * 2048 + (n + 1) * 512],
                                                qtmp1[:, :], qtmp2[:, :],
                                                ALU.add).then_inc(sems["q"], 1)
                            if h == 0:
                                for hh in range(H):
                                    for cp in range(NCORES):
                                        li = hh * NCORES + cp
                                        pslot = pfx[:, (li % 2) * 512:
                                                    (li % 2 + 1) * 512]
                                        v.wait_ge(sems["st"],
                                                  st0 + ((li // 2) + 1) * 32)
                                        dst = tinit_all[:, hh * 512:(hh + 1) * 512]
                                        wv = wpref_sb[:, cp * 8 + hh:cp * 8 + hh + 1]
                                        if cp == 0:
                                            v.tensor_scalar(
                                                dst, pslot, wv, None,
                                                ALU.mult).then_inc(sems["tst"], 1)
                                        else:
                                            v.drain()
                                            v.scalar_tensor_tensor(
                                                dst, pslot, wv, dst, ALU.mult,
                                                ALU.add).then_inc(sems["tst"], 1)
                            if not DO_CHUNKS:
                                continue
                            v.drain()
                            v.tensor_copy(tstate[:, :],
                                          tinit_all[:, h * 512:(h + 1) * 512])
                            v.tensor_copy(tstate_bf[:, :],
                                          tinit_all[:, h * 512:(h + 1) * 512])
                            v.drain()
                            v.sem_inc(sems["tst"], 1)
                            if h > 0:
                                v.wait_ge(sems["gp"], gp0 + h * 3)
                            for ci in range(2):
                                for b in range(B):
                                    cx = ci * B + b
                                    gx = h * 8 + cx
                                    v.wait_ge(sems["sm"], sm0 + gx + 1)
                                    if gx >= 2:
                                        v.wait_ge(sems["ym"], ym0 + gx - 1)
                                    pscx = psc if gx % 2 == 0 else psc2
                                    smo = (gx % 2) * 512
                                    v.tensor_tensor(sm_sb[:, smo:smo + 256],
                                                    pscx[:, 0:256],
                                                    m0_sb[:, :], ALU.mult)
                                    v.tensor_tensor(sm_sb[:, smo + 256:smo + 512],
                                                    pscx[:, 256:512],
                                                    m1_sb[:, :], ALU.mult)
                                    v.drain()
                                    v.sem_inc(sems["sc"], 1)
                                    v.wait_ge(sems["ym"], ym0 + gx + 1)
                                    for qs in range(2):
                                        pt = ci * 2 + qs
                                        ins = v.tensor_scalar(
                                            ystage[:, cx * 256 + qs * 128:
                                                   cx * 256 + (qs + 1) * 128],
                                            py12[:, qs * 128:(qs + 1) * 128],
                                            ysc_sb[:, h * 4 + pt:h * 4 + pt + 1],
                                            None, ALU.mult)
                                    ins.then_inc(sems["tp"], 1)
                                    gCh = float(GAMMAS[h] ** CH)
                                    col = (b * 2 + ci) * 128
                                    v.scalar_tensor_tensor(
                                        tstate[:, b * 128:(b + 1) * 128],
                                        tstate[:, b * 128:(b + 1) * 128], gCh,
                                        i2[:, col:col + 128], ALU.mult, ALU.add)
                                    v.drain()
                                    v.tensor_copy(tstate_bf[:, b * 128:(b + 1) * 128],
                                                  tstate[:, b * 128:(b + 1) * 128])
                                    v.drain()
                                    v.sem_inc(sems["tst"], 1)
                            v.wait_ge(sems["gp"], gp0 + h * 3 + 1)
                            v.tensor_reduce(gstat[:, 0:16],
                                            ystage[:, :].rearrange(
                                                "p (s v) -> p s v", s=16, v=128),
                                            AX.X, ALU.add)
                            v.tensor_reduce(gstat[:, 16:32],
                                            ysq[:, :].rearrange(
                                                "p (s v) -> p s v", s=16, v=128),
                                            AX.X, ALU.add)
                            v.drain()
                            v.tensor_scalar(gstat[:, 32:48], gstat[:, 0:16],
                                            1.0 / 128.0, None, ALU.mult)
                            v.tensor_scalar(gstat[:, 48:64], gstat[:, 16:32],
                                            1.0 / 128.0, None, ALU.mult)
                            v.drain()
                            v.tensor_tensor(gstat[:, 64:80], gstat[:, 32:48],
                                            gstat[:, 32:48], ALU.mult)
                            v.drain()
                            v.tensor_tensor(gstat[:, 48:64], gstat[:, 48:64],
                                            gstat[:, 64:80], ALU.subtract)
                            v.drain()
                            v.tensor_scalar(gstat[:, 48:64], gstat[:, 48:64],
                                            0.0, EPS, ALU.max, ALU.add)
                            v.drain()
                            v.reciprocal(gstat[:, 64:80], gstat[:, 48:64])
                            v.drain()
                            v.sem_inc(sems["ev"], 1)
                            v.wait_ge(sems["gp"], gp0 + h * 3 + 2)
                            v.drain()
                            v.tensor_tensor(gstat[:, 96:112], gstat[:, 32:48],
                                            gstat[:, 80:96], ALU.mult)
                            v.drain()
                            v.tensor_scalar(gstat[:, 96:112], gstat[:, 96:112],
                                            -1.0, None, ALU.mult)
                            v.drain()
                            v.sem_inc(sems["ev"], 1)

                    @blk.scalar
                    def _(s):
                        for h in range(NH if DO_CHUNKS else 0):
                            s.wait_ge(sems["tp"], tp0 + h * 8 + 8)
                            s.activation(ysq[:, :], ystage[:, :], ACT.Square)
                            s.drain()
                            s.sem_inc(sems["gp"], 1)
                            s.wait_ge(sems["ev"], ev0 + h * 28 + 27)
                            s.drain()
                            s.activation(gstat[:, 80:96], gstat[:, 64:80],
                                         ACT.Sqrt)
                            s.drain()
                            s.sem_inc(sems["gp"], 1)
                            s.wait_ge(sems["ev"], ev0 + h * 28 + 28)
                            s.drain()
                            for ci in range(2):
                                for b in range(B):
                                    cx = ci * B + b
                                    for qs in range(2):
                                        sub = cx * 2 + qs
                                        tt = b * 4 + ci * 2 + qs
                                        ins = s.activation(
                                            yn[:, tt * 1024 + h * 128:
                                               tt * 1024 + (h + 1) * 128],
                                            ystage[:, sub * 128:(sub + 1) * 128],
                                            ACT.Identity,
                                            bias=gstat[:, 96 + sub:97 + sub],
                                            scale=gstat[:, 80 + sub:81 + sub])
                            s.drain()
                            s.sem_inc(sems["gp"], 1)

        cnt["cc"] += 1
        cnt["st"] += 64 * 16
        cnt["tst"] += 64 + (NH * 9 if DO_CHUNKS else 0)
        cnt["cst"] += 106 * 16
        cnt["wld"] += NH * 32
        cnt["mm"] += NH * MMH
        cnt["ev"] += NH * EVH
        cnt["aux"] += NH * 8 if DO_INC else 0
        cnt["q"] += NH * 4 if DO_QP else 0
        if DO_CHUNKS:
            cnt["sm"] += NH * 8
            cnt["ym"] += NH * 8
            cnt["sc"] += NH * 8
            cnt["tp"] += NH * 8
            cnt["gp"] += NH * 3
        if SUBSTOP.startswith("A2"):
            return
        if DBG:
            dby = nc.dram_tensor("dbg_yn", [128, NT * 1024], F32,
                                 kind="ExternalOutput").ap()
            dbt = nc.dram_tensor("dbg_tinit", [128, H * 512], F32,
                                 kind="ExternalOutput").ap()
            dbk = nc.dram_tensor("dbg_ktm", [128, 2048], F32,
                                 kind="ExternalOutput").ap()
            dbq = nc.dram_tensor("dbg_qfm", [128, 4096], F32,
                                 kind="ExternalOutput").ap()
            dbv = nc.dram_tensor("dbg_vtm", [128, 4096], F32,
                                 kind="ExternalOutput").ap()
            dbkf = nc.dram_tensor("dbg_kfm", [128, 4096], F32,
                                  kind="ExternalOutput").ap()
            dbi = nc.dram_tensor("dbg_i2", [128, 1024], F32,
                                 kind="ExternalOutput").ap()
            dbsm = nc.dram_tensor("dbg_sm", [128, 512], F32,
                                  kind="ExternalOutput").ap()
            dbys = nc.dram_tensor("dbg_ysb", [128, 256], F32,
                                  kind="ExternalOutput").ap()
            dbgl = nc.dram_tensor("dbg_gl", [128, 16], F32,
                                  kind="ExternalOutput").ap()
            dbts = nc.dram_tensor("dbg_ts", [128, 512], F32,
                                  kind="ExternalOutput").ap()
            with nc.Block("dbg_a2") as blk:
                @blk.gpsimd
                def _(g):
                    for dst, srcb in [(dby, yn[:, :]), (dbt, tinit_all[:, :]),
                                      (dbk, ktm[:, :]), (dbq, qfm[:, :]),
                                      (dbv, vtm2[:, :]), (dbkf, kfm2[:, :]),
                                      (dbi, i2[:, :]), (dbsm, sm_sb[:, :]),
                                      (dbys, ysb[:, :]), (dbgl, gl[:, :]),
                                      (dbts, tstate_bf[:, :])]:
                        g.dma_start(out=dst, in_=srcb).then_inc(sems["st"], 16)
                    g.wait_ge(sems["st"], cnt["st"] + 11 * 16)
            cnt["st"] += 11 * 16

        mm0, ev0, gp0, aux0, wld0 = (cnt["mm"], cnt["ev"], cnt["gp"],
                                     cnt["aux"], cnt["wld"])
        lda3 = cnt["ld"]
        with ExitStack() as a3s:
            wg_sb = a3s.enter_context(nc.sbuf_tensor("wg_sb", [128, 8192], BF16))
            wo_sb = a3s.enter_context(nc.sbuf_tensor("wo_sb", [128, 8192], BF16))
            gsil2 = a3s.enter_context(nc.sbuf_tensor("gsil2", [128, 2048], BF16))
            tnorm2 = a3s.enter_context(nc.sbuf_tensor("tnorm2", [128, 2048], BF16))
            mfm2 = a3s.enter_context(nc.sbuf_tensor("mfm2", [128, 2048], BF16))
            with (nc.psum_tensor("pg", [128, 1024], F32) as pg,
                  nc.psum_tensor("pmt", [128, 1024], BF16) as pmt,
                  nc.psum_tensor("po_a", [128, 1024], F32) as po_a,
                  nc.psum_tensor("po_b", [128, 1024], F32) as po_b):
                qa2 = cnt["cst"]
                with nc.Block("A3") as blk:
                    @blk.gpsimd
                    def _(g):
                        for m in range(32):
                            g.dma_start(out=T["h2t_bf"][m].rearrange("p n -> (p n)"),
                                        in_=T["h2t"][m].rearrange("p n -> (p n)")
                                        ).then_inc(sems["cst"], 16)

                    @blk.sync
                    def _(s):
                        s.wait_ge(sems["cst"], qa2 - 104 * 16)
                        s.dma_start(out=wg_sb[:, :], in_=T["wg_bf"]).then_inc(sems["ld"], 16)
                        s.dma_start(out=wo_sb[:, :], in_=T["wo_bf"]).then_inc(sems["ld"], 16)

                    @blk.tensor
                    def _(t):
                        t.wait_ge(sems["ld"], lda3 + 32)
                        for tt in range(NT):
                            sl = tt % 2
                            if tt > 0:
                                t.wait_ge(sems["gp"], gp0 + 4 * (tt - 1) + 1)
                            for half in range(2):
                                for dc in range(8):
                                    ins = t.matmul(
                                        pg[:, half * 512:(half + 1) * 512],
                                        kb.fmbuf[:, dc * 2048 + tt * 128:
                                                 dc * 2048 + tt * 128 + 128],
                                        wg_sb[:, dc * 1024 + half * 512:
                                              dc * 1024 + (half + 1) * 512],
                                        start=(dc == 0), stop=(dc == 7))
                            ins.then_inc(sems["mm"], 1)
                            t.wait_ge(sems["ev"], ev0 + tt * 3 + 2)
                            if tt > 0:
                                t.wait_ge(sems["ev"], ev0 + (tt - 1) * 3 + 3)
                            for dc in range(8):
                                ins = t.transpose(pmt[:, dc * 128:(dc + 1) * 128],
                                                  tnorm2[:, sl * 1024 + dc * 128:
                                                         sl * 1024 + (dc + 1) * 128],
                                                  kb.idb[:, :])
                            ins.then_inc(sems["mm"], 1)
                            t.wait_ge(sems["ev"], ev0 + tt * 3 + 3)
                            if tt >= 2:
                                t.wait_ge(sems["aux"], aux0 + (tt - 2) * 3 + 1)
                            po = po_a if sl == 0 else po_b
                            for half in range(2):
                                for dc in range(8):
                                    ins = t.matmul(
                                        po[:, half * 512:(half + 1) * 512],
                                        mfm2[:, sl * 1024 + dc * 128:
                                             sl * 1024 + (dc + 1) * 128],
                                        wo_sb[:, dc * 1024 + half * 512:
                                              dc * 1024 + (half + 1) * 512],
                                        start=(dc == 0), stop=(dc == 7))
                            ins.then_inc(sems["mm"], 1)

                    @blk.scalar
                    def _(s):
                        for tt in range(NT):
                            sl = tt % 2
                            s.wait_ge(sems["mm"], mm0 + tt * 3 + 1)
                            if tt >= 2:
                                s.wait_ge(sems["ev"], ev0 + (tt - 2) * 3 + 2)
                            s.activation(gsil2[:, sl * 1024:(sl + 1) * 1024],
                                         pg[:, :], ACT.Silu).then_inc(sems["gp"], 1)
                            s.wait_ge(sems["aux"], aux0 + tt * 3 + 1)
                            ln_s1(s, kb, kb.r32[:, sl * 1024:(sl + 1) * 1024],
                                  sl).then_inc(sems["gp"], 1)
                            s.wait_ge(sems["aux"], aux0 + tt * 3 + 2)
                            ln_s2_sqrt(s, kb, sl).then_inc(sems["gp"], 1)
                            s.wait_ge(sems["aux"], aux0 + tt * 3 + 3)
                            ln_apply(s, kb, kb.r32[:, sl * 1024:(sl + 1) * 1024],
                                     kb.stream[:, tt * 1024:(tt + 1) * 1024],
                                     sl).then_inc(sems["gp"], 1)

                    @blk.vector
                    def _(v):
                        for tt in range(NT):
                            sl = tt % 2
                            if tt >= 2:
                                v.wait_ge(sems["mm"], mm0 + 3 * (tt - 2) + 2)
                            v.tensor_tensor(tnorm2[:, sl * 1024:(sl + 1) * 1024],
                                            yn[:, tt * 1024:(tt + 1) * 1024],
                                            gnw_sb[:, :], ALU.mult)
                            v.drain()
                            v.tensor_tensor(tnorm2[:, sl * 1024:(sl + 1) * 1024],
                                            tnorm2[:, sl * 1024:(sl + 1) * 1024],
                                            gnb_sb[:, :],
                                            ALU.add).then_inc(sems["ev"], 1)
                            v.wait_ge(sems["gp"], gp0 + tt * 4 + 1)
                            v.drain()
                            v.tensor_tensor(tnorm2[:, sl * 1024:(sl + 1) * 1024],
                                            gsil2[:, sl * 1024:(sl + 1) * 1024],
                                            tnorm2[:, sl * 1024:(sl + 1) * 1024],
                                            ALU.mult).then_inc(sems["ev"], 1)
                            v.wait_ge(sems["mm"], mm0 + tt * 3 + 2)
                            v.tensor_copy(mfm2[:, sl * 1024:(sl + 1) * 1024],
                                          pmt[:, :]).then_inc(sems["ev"], 1)
                            v.wait_ge(sems["mm"], mm0 + tt * 3 + 3)
                            if tt >= 2:
                                v.wait_ge(sems["gp"], gp0 + (tt - 2) * 4 + 4)
                            po = po_a if sl == 0 else po_b
                            v.tensor_tensor(kb.r32[:, sl * 1024:(sl + 1) * 1024],
                                            po[:, :],
                                            kb.stream[:, tt * 1024:(tt + 1) * 1024],
                                            ALU.add)
                            v.drain()
                            ln_v1(v, kb, kb.r32[:, sl * 1024:(sl + 1) * 1024],
                                  sl).then_inc(sems["aux"], 1)
                            v.wait_ge(sems["gp"], gp0 + tt * 4 + 2)
                            ln_v2(v, kb, sl).then_inc(sems["aux"], 1)
                            v.wait_ge(sems["gp"], gp0 + tt * 4 + 3)
                            ln_v3(v, kb, sl).then_inc(sems["aux"], 1)

        cnt["mm"] += NT * 3
        cnt["ev"] += NT * 3
        cnt["gp"] += NT * 4
        cnt["aux"] += NT * 3
        cnt["ld"] += 32
        cnt["cst"] += 32 * 16


def pp_slice(ptr, tt):
    return ptr[:, tt * 128:(tt + 1) * 128]







def emit_phase_b(kb):
    nc, sems, cnt, T = kb.nc, kb.sems, kb.cnt, kb.T

    emit_tm_to_fm(kb, "B0")

    with ExitStack() as bes:
        sb = lambda name, shp, dt=BF16: bes.enter_context(nc.sbuf_tensor(name, shp, dt))
        X1 = sb("X1", [128, 32 * 512])
        X2 = sb("X2", [128, 32 * 512])
        w12 = sb("w12", [128, 4 * 1024])
        wh = sb("wh", [128, 2 * 4096])
        w1s_sb = sb("w1s_sb", [128, 32], F32)
        w2s_sb = sb("w2s_sb", [128, 32], F32)
        h1s_sb = sb("h1s_sb", [128, 32], F32)
        h2s_sb = sb("h2s_sb", [128, 32], F32)
        w3s_sb = sb("w3s_sb", [128, 8], F32)
        t1s = sb("t1s", [128, 512])
        t2s = sb("t2s", [128, 512])
        ofm = sb("ofm", [128, 8 * 512])

        ld0 = cnt["ld"]
        with nc.Block("Bld") as blk:
            @blk.sync
            def _(s):
                for ap, src in [(w1s_sb, "w1s"), (w2s_sb, "w2s"), (h1s_sb, "h1s"),
                                (h2s_sb, "h2s"), (w3s_sb, "w3s")]:
                    s.dma_start(out=ap[:, :], in_=T[src]).then_inc(sems["ld"], 16)
                s.wait_ge(sems["ld"], ld0 + 5 * 16)
        cnt["ld"] += 5 * 16

        MMTC = 108
        GPTC = 108
        EVTC = 44
        WLTC = 136 * 16
        mm0, gp0, ev0, wld0, aux0 = (cnt["mm"], cnt["gp"], cnt["ev"], cnt["wlh"],
                                     cnt["aux"])

        def tc_base(tc):
            return (mm0 + tc * MMTC, gp0 + tc * GPTC, ev0 + tc * EVTC,
                    wld0 + tc * WLTC, aux0 + tc * 8)

        with (nc.psum_tensor("p1a", [128, 512], F32) as p1a,
              nc.psum_tensor("p2a", [128, 512], F32) as p2a,
              nc.psum_tensor("p1b", [128, 512], F32) as p1b,
              nc.psum_tensor("p2b", [128, 512], F32) as p2b,
              nc.psum_tensor("pga", [128, 512], F32) as pga,
              nc.psum_tensor("pgb", [128, 512], F32) as pgb,
              nc.psum_tensor("pmt_b", [128, 1024], BF16) as pmt):
            q_all = cnt["cst"]
            with nc.Block("B1") as blk:
                @blk.sync
                def _(g):
                    g.wait_ge(sems["cst"], q_all)
                    for tc in range(4):
                        mmb, gpb, evb, wlb, auxb = tc_base(tc)
                        for m in range(32):
                            g.wait_ge(sems["mm"], mmb + m - 1 if m >= 2 else mmb)
                            g.dma_start(out=w12[:, (m % 2) * 2048:(m % 2) * 2048 + 1024],
                                        in_=T["w1t_bf"][m]).then_inc(sems["wlh"], 16)
                            g.dma_start(out=w12[:, (m % 2) * 2048 + 1024:
                                               (m % 2) * 2048 + 2048],
                                        in_=T["w2t_bf"][m]).then_inc(sems["wlh"], 16)
                            g.wait_ge(sems["wlh"], wlb + (m + 1) * 32)
                        for st, wt, nm, moff in [(2, "h1t_bf", 32, 32),
                                                 (3, "h2t_bf", 32, 64),
                                                 (4, "w3t_bf", 8, 96)]:
                            for m in range(nm):
                                if st == 2 and m < 2:
                                    g.wait_ge(sems["mm"], mmb)
                                else:
                                    g.wait_ge(sems["mm"], mmb + moff + m - 1)
                                g.dma_start(out=wh[:, (m % 2) * 4096:
                                                   (m % 2 + 1) * 4096],
                                            in_=T[wt][m]).then_inc(sems["wlh"], 16)
                                g.wait_ge(sems["wlh"],
                                          wlb + 64 * 16 + (moff - 32 + m + 1) * 16)

                @blk.tensor
                def _(t):
                    for tc in range(4):
                        mmb, gpb, evb, wlb, auxb = tc_base(tc)
                        for m in range(32):
                            t.wait_ge(sems["wlh"], wlb + (m + 1) * 32)
                            if m >= 2:
                                t.wait_ge(sems["gp"], gpb + m - 1)
                                t.wait_ge(sems["ev"], evb + m - 1)
                            p1 = p1a if m % 2 == 0 else p1b
                            p2 = p2a if m % 2 == 0 else p2b
                            for dc in range(8):
                                t.matmul(p1[:, :],
                                         w12[:, (m % 2) * 2048 + dc * 128:
                                             (m % 2) * 2048 + (dc + 1) * 128],
                                         kb.fmbuf[:, dc * 2048 + tc * 512:
                                                  dc * 2048 + (tc + 1) * 512],
                                         start=(dc == 0), stop=(dc == 7))
                            for dc in range(8):
                                ins = t.matmul(p2[:, :],
                                               w12[:, (m % 2) * 2048 + 1024 + dc * 128:
                                                   (m % 2) * 2048 + 1024 + (dc + 1) * 128],
                                               kb.fmbuf[:, dc * 2048 + tc * 512:
                                                        dc * 2048 + (tc + 1) * 512],
                                               start=(dc == 0), stop=(dc == 7))
                            ins.then_inc(sems["mm"], 1)
                        for st, nm, moff, src in [(2, 32, 32, X1), (3, 32, 64, X2),
                                                  (4, 8, 96, X1)]:
                            if st == 2:
                                t.wait_ge(sems["ev"], evb + 32)
                            elif st == 3:
                                t.wait_ge(sems["gp"], gpb + 64)
                            else:
                                t.wait_ge(sems["gp"], gpb + 96)
                            for m in range(nm):
                                t.wait_ge(sems["wlh"], wlb + 64 * 16 +
                                          (moff - 32 + m + 1) * 16)
                                if m >= 2:
                                    if st == 4:
                                        t.wait_ge(sems["ev"], evb + 32 + m - 1)
                                    else:
                                        t.wait_ge(sems["gp"], gpb + moff + m - 1)
                                pg = pga if m % 2 == 0 else pgb
                                for k in range(32):
                                    ins = t.matmul(pg[:, :],
                                                   wh[:, (m % 2) * 4096 + k * 128:
                                                      (m % 2) * 4096 + (k + 1) * 128],
                                                   src[:, k * 512:(k + 1) * 512],
                                                   start=(k == 0), stop=(k == 31))
                                ins.then_inc(sems["mm"], 1)
                        t.wait_ge(sems["ev"], evb + 40)
                        for tl in range(4):
                            tt = tc * 4 + tl
                            if tl > 0:
                                t.wait_ge(sems["ev"], evb + 40 + tl)
                            for m in range(8):
                                ins = t.transpose(pmt[:, m * 128:(m + 1) * 128],
                                                  ofm[:, m * 512 + tl * 128:
                                                      m * 512 + (tl + 1) * 128],
                                                  kb.idb[:, :])
                            ins.then_inc(sems["mm"], 1)

                @blk.scalar
                def _(s):
                    for tc in range(4):
                        mmb, gpb, evb, wlb, auxb = tc_base(tc)
                        for m in range(32):
                            s.wait_ge(sems["mm"], mmb + m + 1)
                            if m >= 1:
                                s.wait_ge(sems["ev"], evb + m)
                            elif tc > 0:
                                s.wait_ge(sems["ev"], evb - EVTC + 32)
                            p1 = p1a if m % 2 == 0 else p1b
                            s.activation(t1s[:, :], p1[:, :], ACT.Silu,
                                         scale=w1s_sb[:, m:m + 1])
                            s.drain()
                            s.sem_inc(sems["gp"], 1)
                        for st, nm, moff, dst, ssb in [(2, 32, 32, X2, h1s_sb),
                                                       (3, 32, 64, X1, h2s_sb)]:
                            for m in range(nm):
                                s.wait_ge(sems["mm"], mmb + moff + m + 1)
                                pg = pga if m % 2 == 0 else pgb
                                s.activation(dst[:, m * 512:(m + 1) * 512], pg[:, :],
                                             ACT.Gelu, scale=ssb[:, m:m + 1])
                                s.drain()
                                s.sem_inc(sems["gp"], 1)
                        for tl in range(4):
                            tt = tc * 4 + tl
                            s.wait_ge(sems["aux"], auxb + tl * 2 + 1)
                            ln_s1(s, kb, kb.r32[:, 0:1024]).then_inc(sems["gp"], 1)
                            s.wait_ge(sems["aux"], auxb + tl * 2 + 2)
                            ln_s2_sqrt(s, kb).then_inc(sems["gp"], 1)
                            s.wait_ge(sems["ev"], evb + 41 + tl)
                            ln_apply(s, kb, kb.r32[:, 0:1024],
                                     kb.stream[:, tt * 1024:(tt + 1) * 1024]
                                     ).then_inc(sems["gp"], 1)

                @blk.vector
                def _(v):
                    for tc in range(4):
                        mmb, gpb, evb, wlb, auxb = tc_base(tc)
                        for m in range(32):
                            v.wait_ge(sems["mm"], mmb + m + 1)
                            v.wait_ge(sems["gp"], gpb + m + 1)
                            p2 = p2a if m % 2 == 0 else p2b
                            v.tensor_scalar(t2s[:, :], p2[:, :],
                                            w2s_sb[:, m:m + 1], None, ALU.mult)
                            v.drain()
                            v.tensor_tensor(X1[:, m * 512:(m + 1) * 512],
                                            t1s[:, :], t2s[:, :], ALU.mult)
                            v.drain()
                            v.sem_inc(sems["ev"], 1)
                        for m in range(8):
                            v.wait_ge(sems["mm"], mmb + 96 + m + 1)
                            pg = pga if m % 2 == 0 else pgb
                            v.tensor_scalar(ofm[:, m * 512:(m + 1) * 512], pg[:, :],
                                            w3s_sb[:, m:m + 1], None,
                                            ALU.mult).then_inc(sems["ev"], 1)
                        for tl in range(4):
                            tt = tc * 4 + tl
                            v.wait_ge(sems["mm"], mmb + 104 + tl + 1)
                            if tl > 0:
                                v.wait_ge(sems["gp"], gpb + 96 + 3 * tl)
                            elif tc > 0:
                                v.wait_ge(sems["gp"], gpb)
                            v.tensor_tensor(kb.r32[:, 0:1024], pmt[:, :],
                                            kb.stream[:, tt * 1024:(tt + 1) * 1024],
                                            ALU.add)
                            v.drain()
                            ln_v1(v, kb, kb.r32[:, 0:1024]).then_inc(sems["aux"], 1)
                            v.wait_ge(sems["gp"], gpb + 96 + tl * 3 + 1)
                            ln_v2(v, kb).then_inc(sems["aux"], 1)
                            v.wait_ge(sems["gp"], gpb + 96 + tl * 3 + 2)
                            ln_v3(v, kb).then_inc(sems["ev"], 1)

        cnt["mm"] += 4 * MMTC
        cnt["gp"] += 4 * GPTC
        cnt["ev"] += 4 * EVTC
        cnt["wlh"] += 4 * WLTC
        cnt["aux"] += 4 * 8

def emit_phase_c(kb):
    nc, sems, cnt, T = kb.nc, kb.sems, kb.cnt, kb.T

    emit_tm_to_fm(kb, "C0")

    ccst0 = cnt["cst"]
    kb.epsb = kb.es.enter_context(nc.sbuf_tensor("epsb", [128, 1], F32))
    with nc.Block("Cpre") as blk:
        @blk.gpsimd
        def _(g):
            g.memset(kb.epsb[:, :], 1e-6)
            g.dma_start(out=T["wkvq2_bf"], in_=T["wkvq2"]).then_inc(sems["cst"], 16)
            g.wait_ge(sems["cst"], ccst0 + 16)
    cnt["cst"] += 16

    with ExitStack() as ces:
        sb = lambda name, shp, dt=F32: ces.enter_context(nc.sbuf_tensor(name, shp, dt))
        Ssb = sb("Ssb", [128, 8192])
        scr = sb("scr", [128, 8192])
        ksb = sb("ksb", [128, 1024]); vsb = sb("vsb", [128, 1024])
        qsb = sb("qsb", [128, 1024]); knsb = sb("knsb", [128, 1024])
        dsb = sb("dsb", [128, 1024])
        sqscr = sb("sqscr", [128, 1024])
        Sqb = sb("Sqb", [128, 1024])
        scr2 = sb("scr2", [128, 1024])
        silscr = sb("silscr", [128, 1024])
        ssv = sb("ssv", [128, 128]); ssr = sb("ssr", [128, 128])
        invn = sb("invn", [128, 128])
        wslc = ces.enter_context(nc.sbuf_tensor("wslc", [128, 2 * 3072], BF16))

        GL = slice(0, 43 * 64)
        GR = slice(43 * 64, 8192)
        GLK = slice(0, 43 * 8)
        GRK = slice(43 * 8, 1024)
        GLI = slice(0, 43 * 8)
        GRI = slice(43 * 8, 1024)
        mm0, ev0, gp0, sc0, tp0, aux0, wld0, ld0 = (
            cnt["mm"], cnt["ev"], cnt["gp"], cnt["sc"], cnt["tp"], cnt["aux"],
            cnt["wld"], cnt["ld"])

        def sc_ix(it, step):
            qt, t = it // 4, it % 4
            return sc0 + qt * 21 + 1 + t * 5 + step

        with (nc.psum_tensor("pk_c", [128, 1024], F32) as pk,
              nc.psum_tensor("pv_c", [128, 1024], F32) as pv,
              nc.psum_tensor("pq_c", [128, 1024], F32) as pq):
            with nc.Block("C1") as blk:
                @blk.sync
                def _(s):
                    s.wait_ge(sems["cst"], ccst0 + 16)
                    for it in range(16):
                        for dc in range(8):
                            if it * 8 + dc >= 2:
                                s.wait_ge(sems["mm"], mm0 + it * 8 + dc - 1)
                            s.dma_start(out=wslc[:, (dc % 2) * 3072:(dc % 2 + 1) * 3072],
                                        in_=T["wkvq2_bf"][:, dc * 3072:(dc + 1) * 3072]
                                        ).then_inc(sems["ld"], 16)
                            s.wait_ge(sems["ld"],
                                      ld0 + (it * 8 + dc + 1) * 16)

                @blk.tensor
                def _(t_):
                    for it in range(16):
                        qt, t = it // 4, it % 4
                        tile = t * 4 + qt
                        if it >= 1:
                            t_.wait_ge(sems["ev"], ev0 + (it - 1) * 4 + 1)
                            t_.wait_ge(sems["gp"], gp0 + (it - 1) * 7 + 1)
                        for dc in range(8):
                            t_.wait_ge(sems["ld"], ld0 + (it * 8 + dc + 1) * 16)
                            lhs = kb.fmbuf[:, dc * 2048 + tile * 128:
                                           dc * 2048 + (tile + 1) * 128]
                            for wi, pp in [(0, pk), (1, pv), (2, pq)]:
                                for half in range(2):
                                    ins = t_.matmul(
                                        pp[:, half * 512:(half + 1) * 512],
                                        lhs,
                                        wslc[:, (dc % 2) * 3072 + wi * 1024 +
                                             half * 512:(dc % 2) * 3072 + wi * 1024 +
                                             (half + 1) * 512],
                                        start=(dc == 0), stop=(dc == 7))
                            ins.then_inc(sems["mm"], 1)

                @blk.vector
                def _(v):
                    for it in range(16):
                        qt, t = it // 4, it % 4
                        tile = t * 4 + qt
                        if t == 0:
                            if qt > 0:
                                v.wait_ge(sems["tp"], tp0 + qt * 4)
                                v.wait_ge(sems["aux"], aux0 + qt * 16)
                            v.memset(Ssb[:, :], 0.0)
                            v.drain()
                            v.sem_inc(sems["sc"], 1)
                        v.wait_ge(sems["mm"], mm0 + it * 8 + 8)
                        v.tensor_copy(ksb[:, :], pk[:, :]).then_inc(sems["ev"], 1)
                        v.wait_ge(sems["gp"], gp0 + it * 7 + 2)
                        v.tensor_reduce(ssv[:, :],
                                        sqscr[:, :].rearrange("p (nb j) -> p nb j",
                                                              nb=128, j=8),
                                        AX.X, ALU.add)
                        v.drain()
                        v.sem_inc(sems["sc"], 1)
                        v.wait_ge(sems["gp"], gp0 + it * 7 + 3)
                        v.reciprocal(invn[:, :], ssr[:, :])
                        v.drain()
                        kn_in1 = invn[:, :].rearrange("p nb -> p nb", nb=128
                                                      ) if False else \
                            invn[:, :].unsqueeze(2).broadcast_to([128, 128, 8])
                        v.tensor_tensor(knsb[:, :].rearrange("p (nb j) -> p nb j",
                                                             nb=128, j=8),
                                        ksb[:, :].rearrange("p (nb j) -> p nb j",
                                                            nb=128, j=8),
                                        kn_in1, ALU.mult)
                        v.drain()
                        v.sem_inc(sems["sc"], 1)
                        v.wait_ge(sems["tp"], tp0 + it)
                        v.tensor_tensor(scr[:, GR].rearrange(
                                            "p (nb i j) -> p nb i j", nb=85, i=8, j=8),
                                        Ssb[:, GR].rearrange(
                                            "p (nb i j) -> p nb i j", nb=85, i=8, j=8),
                                        knsb[:, GRK].rearrange(
                                            "p (nb j) -> p nb j", nb=85, j=8)
                                        .unsqueeze(2).broadcast_to([128, 85, 8, 8]),
                                        ALU.mult)
                        v.drain()
                        v.wait_ge(sems["aux"], aux0 + it * 4 + 1)
                        v.tensor_reduce(dsb[:, :].rearrange("p (nb i) -> p nb i",
                                                            nb=128, i=8),
                                        scr[:, :].rearrange("p (nb i j) -> p nb i j",
                                                            nb=128, i=8, j=8),
                                        AX.X, ALU.add)
                        v.drain()
                        v.tensor_tensor(dsb[:, :], vsb[:, :], dsb[:, :],
                                        ALU.subtract)
                        v.drain()
                        v.sem_inc(sems["sc"], 1)
                        v.tensor_tensor(scr[:, GR].rearrange(
                                            "p (nb i j) -> p nb i j", nb=85, i=8, j=8),
                                        dsb[:, GRI].rearrange(
                                            "p (nb i) -> p nb i", nb=85, i=8)
                                        .unsqueeze(3).broadcast_to([128, 85, 8, 8]),
                                        knsb[:, GRK].rearrange(
                                            "p (nb j) -> p nb j", nb=85, j=8)
                                        .unsqueeze(2).broadcast_to([128, 85, 8, 8]),
                                        ALU.mult)
                        v.drain()
                        v.tensor_tensor(scr[:, GR], Ssb[:, GR], scr[:, GR],
                                        ALU.add)
                        v.drain()
                        v.sem_inc(sems["sc"], 1)
                        v.wait_ge(sems["tp"], tp0 + it + 1)
                        v.tensor_tensor(scr[:, GR].rearrange(
                                            "p (nb i j) -> p nb i j", nb=85, i=8, j=8),
                                        Ssb[:, GR].rearrange(
                                            "p (nb i j) -> p nb i j", nb=85, i=8, j=8),
                                        qsb[:, GRK].rearrange(
                                            "p (nb j) -> p nb j", nb=85, j=8)
                                        .unsqueeze(2).broadcast_to([128, 85, 8, 8]),
                                        ALU.mult)
                        v.drain()
                        v.wait_ge(sems["aux"], aux0 + it * 4 + 4)
                        v.tensor_reduce(Sqb[:, :].rearrange("p (nb i) -> p nb i",
                                                            nb=128, i=8),
                                        scr[:, :].rearrange("p (nb i j) -> p nb i j",
                                                            nb=128, i=8, j=8),
                                        AX.X, ALU.add)
                        v.drain()
                        v.sem_inc(sems["sc"], 1)
                        v.wait_ge(sems["gp"], gp0 + it * 7 + 4)
                        v.tensor_tensor(scr2[:, :], Sqb[:, :], silscr[:, :], ALU.mult)
                        v.drain()
                        if it >= 1:
                            v.wait_ge(sems["gp"], gp0 + (it - 1) * 7 + 7)
                        v.tensor_tensor(kb.r32[:, 0:1024], scr2[:, :],
                                        kb.stream[:, tile * 1024:(tile + 1) * 1024],
                                        ALU.add)
                        v.drain()
                        ln_v1(v, kb, kb.r32[:, 0:1024]).then_inc(sems["ev"], 1)
                        v.wait_ge(sems["gp"], gp0 + it * 7 + 5)
                        ln_v2(v, kb).then_inc(sems["ev"], 1)
                        v.wait_ge(sems["gp"], gp0 + it * 7 + 6)
                        ln_v3(v, kb).then_inc(sems["ev"], 1)

                @blk.scalar
                def _(s):
                    for it in range(16):
                        qt, t = it // 4, it % 4
                        tile = t * 4 + qt
                        s.wait_ge(sems["mm"], mm0 + it * 8 + 8)
                        if it >= 1:
                            s.wait_ge(sems["sc"], sc_ix(it - 1, 5))
                        s.copy(vsb[:, :], pv[:, :])
                        s.copy(qsb[:, :], pq[:, :])
                        s.drain()
                        s.sem_inc(sems["gp"], 1)
                        s.wait_ge(sems["ev"], ev0 + it * 4 + 1)
                        s.activation(sqscr[:, :], ksb[:, :], ACT.Square)
                        s.drain()
                        s.sem_inc(sems["gp"], 1)
                        s.wait_ge(sems["sc"], sc_ix(it, 1))
                        s.activation(ssr[:, :], ssv[:, :], ACT.Sqrt)
                        s.drain()
                        s.activation(ssr[:, :], ssr[:, :], ACT.Identity,
                                     bias=kb.epsb[:, 0:1])
                        s.drain()
                        s.sem_inc(sems["gp"], 1)
                        s.wait_ge(sems["sc"], sc_ix(it, 4))
                        s.wait_ge(sems["aux"], aux0 + it * 4 + 3)
                        s.activation(Ssb[:, :], scr[:, :], ACT.Tanh)
                        s.drain()
                        s.sem_inc(sems["tp"], 1)
                        s.wait_ge(sems["sc"], sc_ix(it, 5))
                        s.activation(silscr[:, :], Sqb[:, :], ACT.Silu)
                        s.drain()
                        s.sem_inc(sems["gp"], 1)
                        s.wait_ge(sems["ev"], ev0 + it * 4 + 2)
                        ln_s1(s, kb, kb.r32[:, 0:1024]).then_inc(sems["gp"], 1)
                        s.wait_ge(sems["ev"], ev0 + it * 4 + 3)
                        ln_s2_sqrt(s, kb).then_inc(sems["gp"], 1)
                        s.wait_ge(sems["ev"], ev0 + it * 4 + 4)
                        ln_apply(s, kb, kb.r32[:, 0:1024],
                                 kb.stream[:, tile * 1024:(tile + 1) * 1024]
                                 ).then_inc(sems["gp"], 1)

                @blk.gpsimd
                def _(g):
                    for it in range(16):
                        g.wait_ge(sems["sc"], sc_ix(it, 2))
                        g.wait_ge(sems["tp"], tp0 + it)
                        g.tensor_tensor(scr[:, GL].rearrange("p (nb i j) -> p nb i j",
                                                             nb=43, i=8, j=8),
                                        Ssb[:, GL].rearrange("p (nb i j) -> p nb i j",
                                                             nb=43, i=8, j=8),
                                        knsb[:, GLK].rearrange("p (nb j) -> p nb j",
                                                               nb=43, j=8)
                                        .unsqueeze(2).broadcast_to([128, 43, 8, 8]),
                                        ALU.mult)
                        g.drain()
                        g.sem_inc(sems["aux"], 1)
                        g.wait_ge(sems["sc"], sc_ix(it, 3))
                        g.tensor_tensor(scr[:, GL].rearrange("p (nb i j) -> p nb i j",
                                                             nb=43, i=8, j=8),
                                        dsb[:, GLI].rearrange("p (nb i) -> p nb i",
                                                              nb=43, i=8)
                                        .unsqueeze(3).broadcast_to([128, 43, 8, 8]),
                                        knsb[:, GLK].rearrange("p (nb j) -> p nb j",
                                                               nb=43, j=8)
                                        .unsqueeze(2).broadcast_to([128, 43, 8, 8]),
                                        ALU.mult)
                        g.drain()
                        g.sem_inc(sems["aux"], 1)
                        g.tensor_tensor(scr[:, GL], Ssb[:, GL], scr[:, GL],
                                        ALU.add)
                        g.drain()
                        g.sem_inc(sems["aux"], 1)
                        g.wait_ge(sems["tp"], tp0 + it + 1)
                        g.tensor_tensor(scr[:, GL].rearrange("p (nb i j) -> p nb i j",
                                                             nb=43, i=8, j=8),
                                        Ssb[:, GL].rearrange("p (nb i j) -> p nb i j",
                                                             nb=43, i=8, j=8),
                                        qsb[:, GLK].rearrange("p (nb j) -> p nb j",
                                                              nb=43, j=8)
                                        .unsqueeze(2).broadcast_to([128, 43, 8, 8]),
                                        ALU.mult)
                        g.drain()
                        g.sem_inc(sems["aux"], 1)

        cnt["mm"] += 16 * 8
        cnt["ev"] += 16 * 4
        cnt["gp"] += 16 * 7
        cnt["sc"] += 4 * 21
        cnt["tp"] += 16
        cnt["aux"] += 16 * 4
        cnt["ld"] += 16 * 8 * 16

def emit_phase_d(kb):
    nc, sems, cnt, T = kb.nc, kb.sems, kb.cnt, kb.T

    emit_tm_to_fm(kb, "D0")

    with ExitStack() as des:
        sb = lambda name, shp, dt=BF16: des.enter_context(nc.sbuf_tensor(name, shp, dt))
        wr_sb = sb("wr_sb", [128, 8192])
        wk_sb = sb("wk_sb", [128, 8192])
        wv_sb = sb("wv_sb", [128, 8192])
        mur_sb = sb("mur_sb", [128, 8], F32)
        muk_sb = sb("muk_sb", [128, 8], F32)
        hflag = sb("hflag", [128, 8], F32)
        halocand = sb("halocand", [128, 256])
        cc_sb = sb("cc_sb", [32, 1024])
        halo_fm = sb("halo_fm", [128, 32], F32)
        xp = sb("xp", [128, 4096])
        xr = sb("xr", [128, 4096])
        xk = sb("xk", [128, 4096])
        dseg = sb("dseg", [128, 512], F32)
        rbuf = sb("rbuf", [128, 4096])
        krelu = sb("krelu", [128, 4096])
        kact = sb("kact", [128, 4096])
        out_fm = sb("out_fm", [128, 4096])
        yout = sb("yout", [128, 2 * 1024], F32)

        wld0, st0, cc0, tp0, ld0 = (cnt["wld"], cnt["st"], cnt["cc"], cnt["tp"],
                                    cnt["ld"])
        mmpre = cnt["mm"]
        phl = des.enter_context(nc.psum_tensor("phl", [128, 256], BF16))
        with nc.Block("Dpre") as blk:
            @blk.sync
            def _(s):
                for ap, src in [(mur_sb, "mur"), (muk_sb, "muk"), (hflag, "haloflag")]:
                    s.dma_start(out=ap[:, :], in_=T[src]).then_inc(sems["ld"], 16)
                s.wait_ge(sems["ld"], ld0 + 3 * 16)

            @blk.gpsimd
            def _(g):
                for ap, src in [(wr_sb, "wr"), (wk_sb, "wkd"), (wv_sb, "wvd")]:
                    g.dma_start(out=ap[:, :], in_=T[src]).then_inc(sems["wld"], 16)
                g.wait_ge(sems["wld"], wld0 + 3 * 16)
                for b in range(B):
                    tile = b * 4 + 3
                    g.dma_start(out=T["cc_h_in"][b:b + 1, :],
                                in_=kb.stream[127:128, tile * 1024:(tile + 1) * 1024]
                                ).then_inc(sems["wld"], 16)
                g.wait_ge(sems["wld"], wld0 + 7 * 16)
                g.collective_compute(
                    "AllGather", ALU.bypass,
                    replica_groups=[list(range(NCORES))],
                    ins=[T["cc_h_in"].opt()], outs=[T["cc_h_out"].opt()],
                ).then_inc(sems["cc"], 1)
                g.wait_ge(sems["cc"], cc0 + 1)
                g.dma_start(out=cc_sb[:, :],
                            in_=T["cc_h_out"]).then_inc(sems["wld"], 16)

            @blk.tensor
            def _(t):
                t.wait_ge(sems["wld"], wld0 + 8 * 16)
                for dc in range(8):
                    ins = t.transpose(phl[:, dc * 32:(dc + 1) * 32],
                                      cc_sb[:, dc * 128:(dc + 1) * 128],
                                      kb.idb[0:32, 0:32])
                ins.then_inc(sems["mm"], 1)

            @blk.vector
            def _(v):
                v.memset(halo_fm[:, :], 0.0)
                v.wait_ge(sems["ld"], ld0 + 3 * 16)
                v.wait_ge(sems["mm"], mmpre + 1)
                v.tensor_copy(halocand[:, :], phl[:, :])
                v.drain()
                for cp in range(NCORES):
                    src_ap = bass.AP(halocand, cp * 4,
                                     [[256, 128], [32, 8], [1, 4]])
                    ins = v.scalar_tensor_tensor(
                        halo_fm[:, :].rearrange("p (dc b) -> p dc b",
                                                dc=8, b=4),
                        src_ap, hflag[:, cp:cp + 1],
                        halo_fm[:, :].rearrange("p (dc b) -> p dc b",
                                                dc=8, b=4),
                        ALU.mult, ALU.add)
                    v.drain()
                ins.then_inc(sems["tp"], 1)
        cnt["ld"] += 3 * 16
        cnt["wld"] += 8 * 16
        cnt["mm"] += 1
        cnt["cc"] += 1
        cnt["tp"] += 1

        MMB, EVB, AUXB, GPB, STB = 28, 21, 8, 28, 64
        mm0, ev0, aux0, gp0, stq0 = (cnt["mm"], cnt["ev"], cnt["aux"], cnt["gp"],
                                     cnt["st"])
        with (nc.psum_tensor("pra", [128, 512], F32) as pra,
              nc.psum_tensor("prb", [128, 512], F32) as prb,
              nc.psum_tensor("pka", [128, 512], F32) as pka,
              nc.psum_tensor("pkb", [128, 512], F32) as pkb,
              nc.psum_tensor("pva", [128, 512], F32) as pva,
              nc.psum_tensor("pvb", [128, 512], F32) as pvb,
              nc.psum_tensor("pmt_d", [128, 1024], BF16) as pmt):
            with nc.Block("D1") as blk:
                @blk.tensor
                def _(t):
                    t.wait_ge(sems["wld"], wld0 + 3 * 16)
                    for b in range(B):
                        mmb, evb, auxb, gpb, stb = (mm0 + b * MMB, ev0 + b * EVB,
                                                    aux0 + b * AUXB, gp0 + b * GPB,
                                                    stq0 + b * STB)
                        t.wait_ge(sems["ev"], evb + 1)
                        for m in range(8):
                            if m >= 2:
                                t.wait_ge(sems["gp"], gpb + m - 1)
                            pr = pra if m % 2 == 0 else prb
                            for kc in range(8):
                                ins = t.matmul(pr[:, :],
                                               wr_sb[:, kc * 1024 + m * 128:
                                                     kc * 1024 + (m + 1) * 128],
                                               xr[:, kc * 512:(kc + 1) * 512],
                                               start=(kc == 0), stop=(kc == 7))
                            ins.then_inc(sems["mm"], 1)
                        for m in range(8):
                            if m >= 2:
                                t.wait_ge(sems["gp"], gpb + 8 + m - 1)
                            pk = pka if m % 2 == 0 else pkb
                            for kc in range(8):
                                ins = t.matmul(pk[:, :],
                                               wk_sb[:, kc * 1024 + m * 128:
                                                     kc * 1024 + (m + 1) * 128],
                                               xk[:, kc * 512:(kc + 1) * 512],
                                               start=(kc == 0), stop=(kc == 7))
                            ins.then_inc(sems["mm"], 1)
                        t.wait_ge(sems["ev"], evb + 9)
                        for m in range(8):
                            if m >= 2:
                                t.wait_ge(sems["ev"], evb + 9 + m - 1)
                            pv = pva if m % 2 == 0 else pvb
                            for kc in range(8):
                                ins = t.matmul(pv[:, :],
                                               wv_sb[:, kc * 1024 + m * 128:
                                                     kc * 1024 + (m + 1) * 128],
                                               kact[:, kc * 512:(kc + 1) * 512],
                                               start=(kc == 0), stop=(kc == 7))
                            ins.then_inc(sems["mm"], 1)
                        t.wait_ge(sems["ev"], evb + 17)
                        for tl in range(4):
                            if tl > 0:
                                t.wait_ge(sems["ev"], evb + 17 + tl)
                            for m in range(8):
                                ins = t.transpose(pmt[:, m * 128:(m + 1) * 128],
                                                  out_fm[:, m * 512 + tl * 128:
                                                         m * 512 + (tl + 1) * 128],
                                                  kb.idb[:, :])
                            ins.then_inc(sems["mm"], 1)

                @blk.vector
                def _(v):
                    for b in range(B):
                        mmb, evb, auxb, gpb, stb = (mm0 + b * MMB, ev0 + b * EVB,
                                                    aux0 + b * AUXB, gp0 + b * GPB,
                                                    stq0 + b * STB)
                        v.wait_ge(sems["tp"], tp0 + 1)
                        if b > 0:
                            v.wait_ge(sems["mm"], mmb - MMB + 24)
                        for dc in range(8):
                            v.tensor_copy(xp[:, dc * 512:dc * 512 + 1],
                                          halo_fm[:, dc * 4 + b:dc * 4 + b + 1])
                            v.tensor_copy(xp[:, dc * 512 + 1:(dc + 1) * 512],
                                          kb.fmbuf[:, dc * 2048 + b * 512:
                                                   dc * 2048 + b * 512 + 511])
                            v.drain()
                            v.tensor_tensor(dseg[:, :],
                                            kb.fmbuf[:, dc * 2048 + b * 512:
                                                     dc * 2048 + (b + 1) * 512],
                                            xp[:, dc * 512:(dc + 1) * 512],
                                            ALU.subtract)
                            v.drain()
                            v.scalar_tensor_tensor(xr[:, dc * 512:(dc + 1) * 512],
                                                   dseg[:, :],
                                                   mur_sb[:, dc:dc + 1],
                                                   xp[:, dc * 512:(dc + 1) * 512],
                                                   ALU.mult, ALU.add)
                            ins = v.scalar_tensor_tensor(
                                xk[:, dc * 512:(dc + 1) * 512], dseg[:, :],
                                muk_sb[:, dc:dc + 1],
                                xp[:, dc * 512:(dc + 1) * 512],
                                ALU.mult, ALU.add)
                        v.drain()
                        v.sem_inc(sems["ev"], 1)
                        for m in range(8):
                            v.wait_ge(sems["gp"], gpb + 8 + m + 1)
                            v.tensor_tensor(kact[:, m * 512:(m + 1) * 512],
                                            krelu[:, m * 512:(m + 1) * 512],
                                            krelu[:, m * 512:(m + 1) * 512],
                                            ALU.mult)
                            v.drain()
                            v.sem_inc(sems["ev"], 1)
                        for m in range(8):
                            v.wait_ge(sems["mm"], mmb + 16 + m + 1)
                            v.wait_ge(sems["gp"], gpb + m + 1)
                            pv = pva if m % 2 == 0 else pvb
                            v.tensor_tensor(out_fm[:, m * 512:(m + 1) * 512],
                                            pv[:, :], rbuf[:, m * 512:(m + 1) * 512],
                                            ALU.mult).then_inc(sems["ev"], 1)
                        for tl in range(4):
                            tile = b * 4 + tl
                            v.wait_ge(sems["mm"], mmb + 24 + tl + 1)
                            if tl > 0:
                                v.wait_ge(sems["gp"], gpb + 16 + 3 * tl)
                            elif b > 0:
                                v.wait_ge(sems["gp"], gpb)
                            v.tensor_tensor(kb.r32[:, 0:1024], pmt[:, :],
                                            kb.stream[:, tile * 1024:(tile + 1) * 1024],
                                            ALU.add)
                            v.drain()
                            ln_v1(v, kb, kb.r32[:, 0:1024]).then_inc(sems["aux"], 1)
                            v.wait_ge(sems["gp"], gpb + 16 + 3 * tl + 1)
                            ln_v2(v, kb).then_inc(sems["aux"], 1)
                            v.wait_ge(sems["gp"], gpb + 16 + 3 * tl + 2)
                            ln_v3(v, kb).then_inc(sems["ev"], 1)

                @blk.scalar
                def _(s):
                    for b in range(B):
                        mmb, evb, auxb, gpb, stb = (mm0 + b * MMB, ev0 + b * EVB,
                                                    aux0 + b * AUXB, gp0 + b * GPB,
                                                    stq0 + b * STB)
                        for m in range(8):
                            s.wait_ge(sems["mm"], mmb + m + 1)
                            pr = pra if m % 2 == 0 else prb
                            s.activation(rbuf[:, m * 512:(m + 1) * 512], pr[:, :],
                                         ACT.Sigmoid)
                            s.drain()
                            s.sem_inc(sems["gp"], 1)
                        for m in range(8):
                            s.wait_ge(sems["mm"], mmb + 8 + m + 1)
                            pk = pka if m % 2 == 0 else pkb
                            s.activation(krelu[:, m * 512:(m + 1) * 512], pk[:, :],
                                         ACT.Relu)
                            s.drain()
                            s.sem_inc(sems["gp"], 1)
                        for tl in range(4):
                            tile = b * 4 + tl
                            s.wait_ge(sems["aux"], auxb + 2 * tl + 1)
                            ln_s1(s, kb, kb.r32[:, 0:1024]).then_inc(sems["gp"], 1)
                            s.wait_ge(sems["aux"], auxb + 2 * tl + 2)
                            ln_s2_sqrt(s, kb).then_inc(sems["gp"], 1)
                            s.wait_ge(sems["ev"], evb + 17 + tl + 1)
                            gl_tl = b * 4 + tl
                            if gl_tl >= 2:
                                s.wait_ge(sems["st"], stq0 + (gl_tl - 1) * 16)
                            ln_apply(s, kb, kb.r32[:, 0:1024],
                                     yout[:, (tl % 2) * 1024:(tl % 2 + 1) * 1024]
                                     ).then_inc(sems["gp"], 1)

                @blk.sync
                def _(s):
                    for b in range(B):
                        gpb = gp0 + b * GPB
                        for tl in range(4):
                            tile = b * 4 + tl
                            s.wait_ge(sems["gp"], gpb + 16 + 3 * tl + 3)
                            s.dma_start(out=kb.y[tile * 128:(tile + 1) * 128, :],
                                        in_=yout[:, (tl % 2) * 1024:(tl % 2 + 1) * 1024]
                                        ).then_inc(sems["st"], 16)
                            s.wait_ge(sems["st"], stq0 + (tile + 1) * 16)
                    s.wait_ge(sems["st"], stq0 + 16 * 16)

        cnt["mm"] += B * MMB
        cnt["ev"] += B * EVB
        cnt["aux"] += B * AUXB
        cnt["gp"] += B * GPB
        cnt["st"] += 16 * 16



def kernel(**inputs):
    import concourse.bass_utils as _bu
    in_maps = host_prepare(inputs)
    nc = build_program()
    res = _bu.run_bass_kernel_spmd(nc, in_maps, core_ids=list(range(NCORES)))
    out = np.empty((B, L, D), np.float32)
    for c in range(NCORES):
        out[:, c * POS:(c + 1) * POS, :] = res.results[c]["y"].reshape(B, POS, D)
    return out
